# version 02
# baseline (speedup 1.0000x reference)
import numpy as np
import ml_dtypes
from contextlib import ExitStack

import concourse.bass as bass
import concourse.mybir as mybir
from concourse.bass import IndirectOffsetOnAxis
from concourse.bass_utils import run_bass_kernel_spmd

F32 = mybir.dt.float32
BF16 = mybir.dt.bfloat16
I32 = mybir.dt.int32
U16 = mybir.dt.uint16
AF = mybir.ActivationFunctionType
ALU = mybir.AluOpType
AX = mybir.AxisListType

NSEQ, TSEQ, D = 4, 2048, 1024
NT = TSEQ // 128
NCH = D // 128
NH, DH = 16, 64
E8, FMOE = 8, 2048
IF, NEXP, PH, NKEY = 4096, 4096, 4, 64
TCORE = NSEQ * TSEQ
EPS = 1e-5
INV_D = 1.0 / D

DEBUG_TAPS = []
STAGE = [99]
SEQLIM = [4]
D2LIM = [3]

(PH_P0, PH_A, PH_B, PH_C1A, PH_C1B, PH_C2, PH_C3, PH_D1, PH_D2, PH_D3) = range(10)
P9 = PH_D3


class SbufPlan:

    def __init__(self, base=18432, limit=228352):
        self.base, self.limit, self.allocs = base, limit, []

    def place(self, name, size, ph0, ph1, align=64):
        size = (size + align - 1) // align * align
        taken = sorted((o, s) for (o, s, p0, p1, _n) in self.allocs
                       if not (p1 < ph0 or p0 > ph1))
        gaps = []
        cur = self.base
        for (o, s) in taken:
            if o > cur:
                gaps.append((cur, o - cur))
            cur = max(cur, o + s)
        if self.limit > cur:
            gaps.append((cur, self.limit - cur))
        fits = sorted((g for g in gaps if g[1] >= size), key=lambda g: g[1])
        assert fits, f"sbuf overflow at {name}: need {size}"
        off = fits[0][0]
        self.allocs.append((off, size, ph0, ph1, name))
        return off


def build_nc(taps=()):
    taps = list(taps)
    nc = bass.Bass("TRN2", target_bir_lowering=False)

    X0 = nc.dram_tensor("x0", [TCORE, D], F32, kind="ExternalInput").ap()
    WR = nc.dram_tensor("wr", [D, 2], F32, kind="ExternalInput").ap()
    BRROW = nc.dram_tensor("brrow", [1, 2], F32, kind="ExternalInput").ap()
    SBROW = nc.dram_tensor("sbrow", [1, 3], F32, kind="ExternalInput").ap()
    G1C = nc.dram_tensor("g1c", [128, NCH], F32, kind="ExternalInput").ap()
    B1C = nc.dram_tensor("b1c", [128, NCH], F32, kind="ExternalInput").ap()
    G2C = nc.dram_tensor("g2c", [128, NCH], F32, kind="ExternalInput").ap()
    B2C = nc.dram_tensor("b2c", [128, NCH], F32, kind="ExternalInput").ap()
    LS1R = nc.dram_tensor("ls1r", [1, D], F32, kind="ExternalInput").ap()
    LS2R = nc.dram_tensor("ls2r", [1, D], F32, kind="ExternalInput").ap()
    CWT = nc.dram_tensor("cwt", [128, NCH * 4], F32, kind="ExternalInput").ap()
    CBC = nc.dram_tensor("cbc", [128, NCH], F32, kind="ExternalInput").ap()
    WQKV = nc.dram_tensor("wqkv", [D, 3 * D], F32, kind="ExternalInput").ap()
    WOUT = nc.dram_tensor("wout", [D, D], F32, kind="ExternalInput").ap()
    WGATE = nc.dram_tensor("wgate", [D, D], F32, kind="ExternalInput").ap()
    WIN = nc.dram_tensor("win", [D, 2 * D], F32, kind="ExternalInput").ap()
    WOS = nc.dram_tensor("wossm", [D, D], F32, kind="ExternalInput").ap()
    WG8 = nc.dram_tensor("wg8", [D, E8], F32, kind="ExternalInput").ap()
    W1 = nc.dram_tensor("w1", [E8, D, FMOE], F32, kind="ExternalInput").ap()
    W3 = nc.dram_tensor("w3", [E8, D, FMOE], F32, kind="ExternalInput").ap()
    W2 = nc.dram_tensor("w2", [E8, FMOE, D], F32, kind="ExternalInput").ap()
    WUPT = nc.dram_tensor("wupT", [D, IF], F32, kind="ExternalInput").ap()
    WDNT = nc.dram_tensor("wdnT", [IF, D], F32, kind="ExternalInput").ap()
    WQT = nc.dram_tensor("wqT", [D, 512], F32, kind="ExternalInput").ap()
    DET = nc.dram_tensor("deT", [D, NEXP], F32, kind="ExternalInput").ap()
    UED = nc.dram_tensor("ue", [NEXP, D], F32, kind="ExternalInput").ap()
    KEYSQ = nc.dram_tensor("keysq", [128, 2 * PH * NKEY], F32, kind="ExternalInput").ap()
    IDENTB = nc.dram_tensor("identb", [128, 128], BF16, kind="ExternalInput").ap()
    IDENTF = nc.dram_tensor("identf", [128, 128], F32, kind="ExternalInput").ap()
    IOTATL = nc.dram_tensor("iotatl", [128, 1], F32, kind="ExternalInput").ap()
    IOTA64 = nc.dram_tensor("iota64", [128, 64], F32, kind="ExternalInput").ap()

    OUT = nc.dram_tensor("out", [TCORE, D], F32, kind="ExternalOutput").ap()

    NIN_D = nc.dram_tensor("nin_d", [TCORE, D], F32, kind="Internal").ap()
    H2_D = nc.dram_tensor("h2_d", [TSEQ, D], F32, kind="Internal").ap()
    X1_D = nc.dram_tensor("x1_d", [TSEQ, D], F32, kind="Internal").ap()
    GDD = nc.dram_tensor("gd", [TSEQ, NEXP], F32, kind="Internal").ap()
    RDD = nc.dram_tensor("rd", [TCORE, NEXP], F32, kind="Internal").ap()
    GD_flat = GDD.rearrange("a (b o) -> (a b) o", o=1)
    RD_flat = RDD.rearrange("a (b o) -> (a b) o", o=1)

    tap_shapes = {
        "normed": [128, NCH * TSEQ], "qf": [128, NCH * TSEQ], "kf": [128, NCH * TSEQ],
        "vtm": [128, NT * NH * 65], "attnO": [128, NCH * TSEQ], "u": [128, NCH * TSEQ],
        "normed2": [128, NCH * TSEQ], "cw": [128, NT * E8], "ws": [128, NT],
        "alpha": [128, NT], "ffm": [128, NCH * TSEQ], "qpk": [128, 4 * TSEQ],
        "moe": [128, NT * D], "sele": [128, NT * PH * 8], "selwt": [128, NT * PH * 8],
    }
    tap_out = {t: nc.dram_tensor("tap_" + t, tap_shapes[t], F32, kind="ExternalOutput").ap()
               for t in taps}

    plan = SbufPlan()

    sb_off = {}

    def mksb(name, shape, dtype, ph0, ph1):
        nb = {F32: 4, BF16: 2, I32: 4, U16: 2}[dtype]
        off = plan.place(name, int(np.prod(shape[1:])) * nb, ph0, ph1)
        sb_off[name] = off
        return nc.alloc_sbuf_tensor_at(name, list(shape), dtype, offset=off)

    def mksb_alias(name, shape, dtype, host, extra_off=0):
        return nc.alloc_sbuf_tensor_at(name, list(shape), dtype,
                                       offset=sb_off[host] + extra_off)

    g1c = mksb("g1c_s", [128, NCH], F32, 0, P9)
    b1c = mksb("b1c_s", [128, NCH], F32, 0, P9)
    g2c = mksb("g2c_s", [128, NCH], F32, 0, P9)
    b2c = mksb("b2c_s", [128, NCH], F32, 0, P9)
    cbc = mksb("cbc_s", [128, NCH], F32, 0, P9)
    cwt = mksb("cwt_s", [128, NCH * 4], F32, 0, P9)
    identb = mksb("identb_s", [128, 128], BF16, 0, P9)
    identf = mksb("identf_s", [128, 128], F32, 0, P9)
    iotatl = mksb("iotatl_s", [128, 1], F32, 0, P9)
    iota64 = mksb("iota64_s", [128, 64], F32, 0, P9)
    ones_b = mksb("ones_b", [1, 128], BF16, 0, P9)
    ones_f = mksb("ones_f", [1, 128], F32, 0, P9)
    wr_s = mksb("wr_s", [128, NCH * 2], BF16, 0, P9)
    wg_s = mksb("wg_s", [128, NCH * E8], BF16, 0, P9)
    keys_s = mksb("keys_s", [128, 2 * PH * NKEY], BF16, 0, P9)
    sbb = mksb("sbb_s", [128, 3], F32, 0, P9)
    brr = mksb("brr_s", [1, 2], BF16, 0, P9)
    lsb = mksb("lsb_s", [128, D], F32, 0 if taps else PH_C1A, P9)
    epsc = mksb("epsc", [128, 1], F32, 0, P9)
    mhalfc = mksb("mhalfc", [128, 1], F32, 0, P9)
    m1c = mksb("m1c", [128, 1], F32, 0, P9)
    onec = mksb("onec", [128, 1], F32, 0, P9)
    st1 = mksb("st1", [128, 1], F32, 0, P9)
    st2 = mksb("st2", [128, 1], F32, 0, P9)
    st3 = mksb("st3", [128, 1], F32, 0, P9)
    st5 = mksb("st5", [128, 1], F32, 0, P9)
    st8 = mksb("st8", [128, 8], F32, 0, P9)
    m8 = mksb("m8", [128, 8], F32, 0, P9)
    t8 = mksb("t8", [128, 8], F32, 0, P9)
    st4w = mksb("st4w", [128, 4], F32, 0, P9)
    al_c = mksb("al_c", [128, NT], F32, 0, P9)
    ws_c = mksb("ws_c", [128, NT], F32, 0, P9)
    cw_c = mksb("cw_c", [128, NT * E8], F32, 0, P9)
    wsl = [mksb(f"wsl{i}", [128, NCH * 128], BF16, 0, P9) for i in range(4)]
    wrh = [mksb(f"wrh{i}", [128, 512], BF16, 0, P9) for i in range(2)]
    zrow = mksb("zrow", [1, 512], F32, 0, P9)
    zbb = mksb("zbb", [64, 512], F32, 0, P9)
    nmrow = mksb("nmrow", [1, TSEQ], BF16, PH_A, PH_B)

    normed = mksb("normed", [128, NCH * TSEQ], BF16, PH_A, PH_C1B)
    attnO = mksb("attnO", [128, NCH * TSEQ], BF16, PH_B, PH_C1B)
    u_fm = mksb("u_fm", [128, NCH * TSEQ + 256], BF16, PH_C1A, PH_C1B)
    normed2 = mksb("normed2", [128, NCH * TSEQ], BF16, PH_C1B, PH_C2)
    moe_acc = mksb("moe_acc", [128, NT * D], F32, PH_C2, PH_C3)
    ffm = mksb("ffm", [128, NCH * TSEQ], BF16, PH_C3, PH_D3)
    ue_s = mksb("ue_s", [128, 32 * 1024], BF16, PH_D1, PH_D3)
    xa = mksb("xa", [128, D], F32, PH_A, PH_C1B)
    xb = mksb("xb", [128, D], F32, PH_A, PH_C1B)
    xsq = mksb("xsq", [128, D], F32, PH_A, PH_C1B)
    zt = mksb("zt", [128, D], BF16, PH_A, PH_C1B)
    vtm = mksb("vtm", [128, NT * NH * 65], BF16, PH_B, PH_B)
    qf = mksb("qf", [128, NCH * TSEQ], BF16, PH_B, PH_B)
    kf = mksb("kf", [128, NCH * TSEQ], BF16, PH_B, PH_B)
    wv0 = mksb_alias("wv0", [128, NCH * 512], BF16, "attnO", 0)
    wv1 = mksb_alias("wv1", [128, NCH * 512], BF16, "attnO", NCH * 512 * 2)
    expT = mksb("expT", [128, 16 * 128], BF16, PH_B, PH_B)
    wgo = [mksb(f"wgo{i}", [128, NCH * 512], BF16, PH_C1B, PH_C1B) for i in range(4)]
    xs_c = mksb("xs_c", [128, TSEQ + 3], F32, PH_C1A, PH_C1A)
    ys_c = mksb("ys_c", [128, TSEQ], F32, PH_C1A, PH_C1A)
    sz_c = mksb("sz_c", [128, TSEQ], F32, PH_C1A, PH_C1A)
    xc4 = mksb("xc4", [128, D], F32, PH_C1B, PH_C1B)
    hfm = mksb("hfm", [128, 16 * 1024], BF16, PH_C2, PH_C2)
    w2r = mksb("w2r", [128, 8 * 1024], BF16, PH_C2, PH_C2)
    msl = mksb("msl", [128, 1024], F32, PH_C2, PH_C2)
    xd1 = mksb("xd1", [128, D], F32, PH_C3, PH_C3)
    xd2 = mksb("xd2", [128, D], F32, PH_C3, PH_C3)
    xd3 = mksb("xd3", [128, D], F32, PH_C3, PH_C3)
    xd4 = mksb("xd4", [128, D], F32, PH_C3, PH_C3)
    ztd = mksb("ztd", [128, D], BF16, PH_C3, PH_C3)
    xsq2 = mksb("xsq2", [128, D], F32, PH_C3, PH_C3)
    qpk = mksb("qpk", [128, 4 * TSEQ], BF16, PH_D1, PH_D2)
    gsp = [mksb(f"gsp{i}", [128, 512], F32, PH_D1, PH_D1) for i in range(2)]
    dsl = [mksb(f"dsl{i}", [128, NCH * 512], BF16, PH_D1, PH_D1) for i in range(2)]
    sims = mksb("sims", [128, 512], F32, PH_D2, PH_D2)
    sel = mksb("sel", [128, 1600], F32, PH_D2, PH_D2)
    selu = mksb("selu", [128, 16], U16, PH_D2, PH_D2)
    seli = mksb("seli", [128, PH * 8], I32, PH_D2, PH_D2)
    selg = mksb("selg", [128, PH * 8], F32, PH_D2, PH_D2)
    selw = mksb("selw", [128, PH * 8], F32, PH_D2, PH_D2)
    wdnr = mksb("wdnr", [128, 32 * 512], BF16, PH_D3, PH_D3)
    rtt = mksb("rtt", [128, 32 * 128], BF16, PH_D3, PH_D3)
    rlb = mksb("rlb", [128, NEXP], BF16, PH_D3, PH_D3)
    hfd = mksb("hfd", [128, 32 * 512], BF16, PH_D3, PH_D3)
    xe1 = mksb("xe1", [128, 512], F32, PH_D3, PH_D3)
    xe2 = mksb("xe2", [128, 512], F32, PH_D3, PH_D3)
    zerf = mksb("zerf", [128, NEXP], F32, PH_P0, PH_P0)

    with ExitStack() as ctx:
        EE = ctx.enter_context
        ps_big = EE(nc.psum_tensor("ps_big", [128, 2048], F32))
        ps_a = EE(nc.psum_tensor("ps_a", [128, 512], F32))
        ps_b = EE(nc.psum_tensor("ps_b", [128, 512], F32))
        ps_t0 = EE(nc.psum_tensor("ps_t0", [128, 256], BF16))
        ps_t1 = EE(nc.psum_tensor("ps_t1", [128, 256], BF16))
        psq = [ps_big.ap()[:, i * 512:(i + 1) * 512] for i in range(4)]

        sems, cnt = {}, {}
        SEMN = ["ev", "vt", "sv", "go", "gsc", "rdz", "mis", "msg", "lsr", "tap",
                "xa", "nsp", "xc4", "h2s", "xdl", "x1s", "ue", "wdnr", "rlb",
                "xe1", "out", "w2r", "dsl0", "dsl1", "gsp0", "gsp1",
                "wsl0", "wsl1", "wsl2", "wsl3", "wv0", "wv1",
                "wgo0", "wgo1", "wgo2", "wgo3",
                "wrh0", "wrh1"]
        for name in SEMN:
            sems[name] = EE(nc.semaphore(name))
            cnt[name] = 0
        block = EE(nc.Block())

        def inc(inst, name, k=1):
            inst.then_inc(sems[name], k)
            cnt[name] += k

        ENG = ["sync", "gpsimd", "vector", "scalar", "tensor"]

        def barrier():
            snap = {n: cnt[n] for n in SEMN}
            for en in ENG:
                def emit(e, en=en):
                    for n in SEMN:
                        if snap[n]:
                            e.wait_ge(sems[n], snap[n])
                getattr(block, en)(emit)

        class Chained:
            PASS = {"wait_ge", "dma_start", "sem_inc", "nop", "engine_nop",
                    "indirect_dma_start"}

            def __init__(self, e, sem):
                self._e, self._sem = e, sem

            def __getattr__(self, name):
                fn = getattr(self._e, name)
                if name in Chained.PASS or name.startswith("_") or not callable(fn):
                    return fn

                def wrapped(*a, **k):
                    self._e.wait_ge(sems[self._sem], cnt[self._sem])
                    r = fn(*a, **k)
                    inc(r, self._sem)
                    return r
                return wrapped

        S, G_, T = block.sync, block.gpsimd, block.tensor

        def V(f):
            block.vector(lambda v: f(Chained(v, "vt")))

        def SC(f):
            block.scalar(lambda sc: f(Chained(sc, "sv")))

        def fmv(buf, c, t0, n):
            return buf.ap()[:, c * TSEQ + t0: c * TSEQ + t0 + n]

        SLOTSEM = {id(wsl[0]): "wsl0", id(wsl[1]): "wsl1",
                   id(wsl[2]): "wsl2", id(wsl[3]): "wsl3",
                   id(wv0): "wv0", id(wv1): "wv1",
                   id(wgo[0]): "wgo0", id(wgo[1]): "wgo1", id(wgo[2]): "wgo2",
                   id(wgo[3]): "wgo3",
                   id(wrh[0]): "wrh0", id(wrh[1]): "wrh1",
                   id(dsl[0]): "dsl0", id(dsl[1]): "dsl1",
                   id(gsp[0]): "gsp0", id(gsp[1]): "gsp1"}

        def ssem(slot):
            return SLOTSEM[id(slot)]

        def wait_dma(e, *streams):
            for st_ in streams:
                if cnt[st_]:
                    e.wait_ge(sems[st_], cnt[st_])

        def load_lhsT(g, slot, wdram, col0, ncol=128):
            inc(g.dma_start(
                out=slot.ap()[:, 0:NCH * ncol].rearrange("p (c n) -> p c n", n=ncol),
                in_=wdram[:, col0:col0 + ncol].rearrange("(c p) n -> p c n", p=128)),
                ssem(slot), 16)

        def load_rhs_blk(g, slot, wdram, n0):
            inc(g.dma_start(
                out=slot.ap().rearrange("p (c n) -> p c n", n=512),
                in_=wdram[:, n0:n0 + 512].rearrange("(c p) n -> p c n", p=128)),
                ssem(slot), 16)

        def load_rhs(g, slot, wdram, c, n0, nw):
            inc(g.dma_start(out=slot.ap()[:, 0:nw],
                            in_=wdram[c * 128:(c + 1) * 128, n0:n0 + nw]), ssem(slot), 16)

        def ln_stats(tile_ap, sq_ap, pre=None):
            def _(v):
                if pre is not None:
                    wait_dma(v, pre)
                v.reduce_sum(out=st1.ap(), in_=tile_ap, axis=AX.X)
            V(_)

            def _(sc):
                sc.wait_ge(sems["vt"], cnt["vt"])
                sc.activation(out=sq_ap, in_=tile_ap, func=AF.Square,
                                  accum_out=st2.ap())
            SC(_)

            def _(v):
                v.wait_ge(sems["sv"], cnt["sv"])
                v.tensor_scalar_mul(out=st1.ap(), in0=st1.ap(), scalar1=INV_D)
                v.tensor_scalar_mul(out=st2.ap(), in0=st2.ap(), scalar1=INV_D)
                v.tensor_tensor(out=st3.ap(), in0=st1.ap(), in1=st1.ap(), op=ALU.mult)
                v.tensor_sub(out=st2.ap(), in0=st2.ap(), in1=st3.ap())
            V(_)

            def _(sc):
                sc.wait_ge(sems["vt"], cnt["vt"])
                sc.activation(out=st3.ap(), in_=st2.ap(), func=AF.Ln, bias=epsc.ap())
                sc.activation(out=st3.ap(), in_=st3.ap(), func=AF.Exp,
                                  scale=mhalfc.ap())
            SC(_)

        def transpose_to_fm(src_ap, dst_buf, t0, gc=None, bc=None):
            for c in range(NCH):
                pst = [ps_t0, ps_t1][c % 2]

                def _(t, c=c, pst=pst):
                    if c < 2:
                        t.wait_ge(sems["vt"], cnt["vt"])
                    else:
                        t.wait_ge(sems["vt"], cnt["vt"] - 1)
                    inc(t.transpose(pst.ap()[:, 0:128], src_ap[:, c * 128:(c + 1) * 128],
                                    identb.ap()), "ev")
                T(_)

                def _(v, c=c, pst=pst):
                    v.wait_ge(sems["ev"], cnt["ev"])
                    dst = fmv(dst_buf, c, t0, 128)
                    if gc is not None:
                        v.tensor_scalar(out=dst, in0=pst.ap()[:, 0:128],
                                            scalar1=gc.ap()[:, c:c + 1],
                                            scalar2=bc.ap()[:, c:c + 1],
                                            op0=ALU.mult, op1=ALU.add)
                    else:
                        v.tensor_copy(out=dst, in_=pst.ap()[:, 0:128])
                V(_)

        def tap_bf16(name, buf, ncols):
            if name not in tap_out:
                return
            barrier()
            for cix in range(ncols // 1024):
                def _(v, cix=cix):
                    wait_dma(v, "tap")
                    v.tensor_copy(out=lsb.ap(),
                                      in_=buf.ap()[:, cix * 1024:(cix + 1) * 1024])
                V(_)

                def _(sync, cix=cix):
                    sync.wait_ge(sems["vt"], cnt["vt"])
                    inc(sync.dma_start(out=tap_out[name][:, cix * 1024:(cix + 1) * 1024],
                                       in_=lsb.ap()), "tap", 16)
                S(_)
            barrier()

        def tap_f32(name, buf, ncols):
            if name not in tap_out:
                return
            barrier()

            def _(sync):
                inc(sync.dma_start(out=tap_out[name][:, 0:ncols],
                                   in_=buf.ap()[:, 0:ncols]), "tap", 16)
            S(_)
            barrier()

        def _(sync):
            for dst, src in [(g1c, G1C), (b1c, B1C), (g2c, G2C), (b2c, B2C),
                             (cbc, CBC), (cwt, CWT), (identb, IDENTB), (identf, IDENTF),
                             (iotatl, IOTATL), (iota64, IOTA64)]:
                inc(sync.dma_start(out=dst.ap(), in_=src), "mis", 16)
            inc(sync.dma_start(out=sbb.ap()[0:1, 0:3], in_=SBROW), "mis", 16)
        S(_)

        def _(g):
            inc(g.dma_start(out=wr_s.ap().rearrange("p (c k) -> p c k", k=2),
                            in_=WR.rearrange("(c p) k -> p c k", p=128)), "msg", 16)
            inc(g.dma_start(out=wg_s.ap().rearrange("p (c k) -> p c k", k=E8),
                            in_=WG8.rearrange("(c p) k -> p c k", p=128)), "msg", 16)
            inc(g.dma_start(out=keys_s.ap(), in_=KEYSQ), "msg", 16)
            inc(g.dma_start(out=brr.ap(), in_=BRROW), "msg", 16)
        G_(_)
        barrier()

        def _(v):
            v.memset(ones_b.ap(), 1.0)
            v.memset(ones_f.ap(), 1.0)
            v.memset(zerf.ap(), 0.0)
            v.memset(epsc.ap(), EPS)
            v.memset(mhalfc.ap(), -0.5)
            v.memset(m1c.ap(), -1.0)
            v.memset(onec.ap(), 1.0)
        V(_)
        barrier()

        def _(t):
            inc(t.matmul(ps_a.ap()[:, 0:3], ones_f.ap()[0:1, 0:128], sbb.ap()[0:1, 0:3],
                         start=True, stop=True), "ev")
        T(_)

        def _(v):
            v.wait_ge(sems["ev"], cnt["ev"])
            v.tensor_copy(out=sbb.ap()[:, 0:3], in_=ps_a.ap()[:, 0:3])
        V(_)

        def _(sc):
            for i in range(TCORE // 128):
                inc(sc.dma_start(out=RDD[i * 128:(i + 1) * 128, :], in_=zerf.ap()),
                    "rdz", 16)
        SC(_)
        barrier()

        def build_ls(src_row):
            def _(sync):
                sync.wait_ge(sems["vt"], cnt["vt"])
                inc(sync.dma_start(out=zrow.ap(), in_=src_row[0:1, 0:512]), "lsr", 16)
                inc(sync.dma_start(out=zbb.ap()[0:1, :], in_=src_row[0:1, 512:1024]),
                    "lsr", 16)
            S(_)

            def _(t):
                wait_dma(t, "lsr")
                t.matmul(ps_a.ap(), ones_f.ap()[0:1, 0:128], zrow.ap(),
                         start=True, stop=True)
                inc(t.matmul(ps_b.ap(), ones_f.ap()[0:1, 0:128], zbb.ap()[0:1, :],
                             start=True, stop=True), "ev")
            T(_)

            def _(v):
                v.wait_ge(sems["ev"], cnt["ev"])
                v.tensor_copy(out=lsb.ap()[:, 0:512], in_=ps_a.ap())
                v.tensor_copy(out=lsb.ap()[:, 512:1024], in_=ps_b.ap())
            V(_)

        def emit_c_phases(s, SOFF):
            for fo in range(NCH):
                def _(g):
                    g.wait_ge(sems["ev"], cnt["ev"])
                    load_lhsT(g, wsl[0], WIN, fo * 128)
                    load_lhsT(g, wsl[1], WIN, 1024 + fo * 128)
                G_(_)

                def _(t):
                    wait_dma(t, "wsl0", "wsl1")
                    t.wait_ge(sems["vt"], cnt["vt"])
                    last = None
                    for tb in range(4):
                        for c in range(NCH):
                            last = t.matmul(psq[tb], wsl[0].ap()[:, c * 128:(c + 1) * 128],
                                            fmv(normed, c, tb * 512, 512),
                                            start=(c == 0), stop=(c == NCH - 1))
                    inc(last, "ev")
                T(_)

                def _(v):
                    v.wait_ge(sems["ev"], cnt["ev"])
                    for tb in range(4):
                        v.tensor_copy(out=xs_c.ap()[:, 3 + tb * 512:3 + (tb + 1) * 512],
                                      in_=psq[tb])
                    v.memset(xs_c.ap()[:, 0:3], 0.0)
                V(_)

                def _(t):
                    t.wait_ge(sems["vt"], cnt["vt"])
                    last = None
                    for tb in range(4):
                        for c in range(NCH):
                            last = t.matmul(psq[tb], wsl[1].ap()[:, c * 128:(c + 1) * 128],
                                            fmv(normed, c, tb * 512, 512),
                                            start=(c == 0), stop=(c == NCH - 1))
                    inc(last, "ev")
                T(_)

                def _(sc):
                    sc.wait_ge(sems["ev"], cnt["ev"])
                    last = None
                    for tb in range(4):
                        last = sc.activation(out=sz_c.ap()[:, tb * 512:(tb + 1) * 512],
                                             in_=psq[tb], func=AF.Sigmoid)
                    pass
                SC(_)

                def _(v, fo=fo):
                    v.wait_ge(sems["sv"], cnt["sv"])
                    v.tensor_scalar_mul(out=ys_c.ap(), in0=xs_c.ap()[:, 0:TSEQ],
                                        scalar1=cwt.ap()[:, fo * 4:fo * 4 + 1])
                    last = None
                    for k in (1, 2, 3):
                        last = v.scalar_tensor_tensor(
                            out=ys_c.ap(), in0=xs_c.ap()[:, k:k + TSEQ],
                            scalar=cwt.ap()[:, fo * 4 + k:fo * 4 + k + 1],
                            in1=ys_c.ap(), op0=ALU.mult, op1=ALU.add)
                    pass
                V(_)

                def _(sc, fo=fo):
                    sc.wait_ge(sems["vt"], cnt["vt"])
                    sc.activation(out=ys_c.ap(), in_=ys_c.ap(), func=AF.Silu,
                                      bias=cbc.ap()[:, fo:fo + 1])
                SC(_)

                def _(v, fo=fo):
                    v.wait_ge(sems["sv"], cnt["sv"])
                    v.tensor_tensor(out=fmv(u_fm, fo, 0, TSEQ), in0=ys_c.ap(),
                                        in1=sz_c.ap(), op=ALU.mult)
                V(_)
            barrier()
            if s == 0:
                tap_bf16("u", u_fm, NCH * TSEQ)

            build_ls(LS1R)

            def _(g):
                for wi, wdram in enumerate([WGATE, WOUT]):
                    for half in range(2):
                        load_rhs_blk(g, wgo[wi * 2 + half], wdram, half * 512)
            G_(_)

            for i in range(NT):
                R0 = SOFF + i * 128

                def _(sync):
                    sync.wait_ge(sems["vt"], cnt["vt"])
                    sync.wait_ge(sems["sv"], cnt["sv"])
                    inc(sync.dma_start(out=xc4.ap(), in_=NIN_D[R0:R0 + 128, :]), "xc4", 16)
                S(_)

                for half in range(2):
                    for c in range(NCH):
                        slot = wrh[c % 2]

                        def _(g, c=c, half=half, slot=slot):
                            g.wait_ge(sems["ev"], cnt["ev"])
                            load_rhs(g, slot, WOS, c, half * 512, 512)
                        G_(_)

                        def _(t, c=c, slot=slot, i=i, half=half):
                            wait_dma(t, ssem(slot))
                            if c == 0:
                                t.wait_ge(sems["vt"], cnt["vt"])
                                for which in range(2):
                                    blk_ = wgo[which * 2 + half]
                                    wait_dma(t, ssem(blk_))
                                    last = None
                                    for cc in range(NCH):
                                        last = t.matmul(
                                            psq[which], fmv([normed, attnO][which],
                                                            cc, i * 128, 128),
                                            blk_.ap()[:, cc * 512:(cc + 1) * 512],
                                            start=(cc == 0), stop=(cc == NCH - 1))
                                    inc(last, "ev")
                            inc(t.matmul(psq[2], fmv(u_fm, c, i * 128, 128),
                                         slot.ap(), start=(c == 0),
                                         stop=(c == NCH - 1)), "ev")
                        T(_)

                    def _(sc, half=half):
                        sc.wait_ge(sems["ev"], cnt["ev"])
                        sc.activation(out=xsq.ap()[:, half * 512:(half + 1) * 512],
                                          in_=psq[0], func=AF.Sigmoid)
                    SC(_)

                    def _(v, half=half, i=i):
                        v.wait_ge(sems["sv"], cnt["sv"])
                        wait_dma(v, "xc4", "h2s")
                        hs = slice(half * 512, (half + 1) * 512)
                        v.tensor_tensor(out=xa.ap()[:, hs], in0=psq[1],
                                        in1=xsq.ap()[:, hs], op=ALU.mult)
                        v.tensor_tensor(out=xsq.ap()[:, hs], in0=xa.ap()[:, hs],
                                        in1=psq[2], op=ALU.subtract)
                        v.scalar_tensor_tensor(out=xsq.ap()[:, hs], in0=xsq.ap()[:, hs],
                                               scalar=al_c.ap()[:, i:i + 1], in1=psq[2],
                                               op0=ALU.mult, op1=ALU.add)
                        v.tensor_tensor(out=xsq.ap()[:, hs], in0=xsq.ap()[:, hs],
                                        in1=lsb.ap()[:, hs], op=ALU.mult)
                        v.tensor_tensor(out=xb.ap()[:, hs], in0=xc4.ap()[:, hs],
                                            in1=xsq.ap()[:, hs], op=ALU.add)
                    V(_)

                def _(sync, i=i):
                    sync.wait_ge(sems["vt"], cnt["vt"])
                    inc(sync.dma_start(out=H2_D[i * 128:(i + 1) * 128, :], in_=xb.ap()),
                        "h2s", 16)
                S(_)

                ln_stats(xb.ap(), xa.ap())

                def _(v):
                    v.wait_ge(sems["sv"], cnt["sv"])
                    v.tensor_scalar(out=zt.ap(), in0=xb.ap(), scalar1=st1.ap(),
                                        scalar2=st3.ap(), op0=ALU.subtract,
                                        op1=ALU.mult)
                V(_)
                transpose_to_fm(zt.ap(), normed2, i * 128, g2c, b2c)

                def _(t, i=i):
                    t.wait_ge(sems["vt"], cnt["vt"])
                    for c in range(NCH):
                        last = t.matmul(ps_a.ap()[:, 0:8], fmv(normed2, c, i * 128, 128),
                                        wg_s.ap()[:, c * 8:(c + 1) * 8],
                                        start=(c == 0), stop=(c == NCH - 1))
                    inc(last, "ev")
                T(_)

                def _(v):
                    v.wait_ge(sems["ev"], cnt["ev"])
                    v.tensor_copy(out=st8.ap(), in_=ps_a.ap()[:, 0:8])
                    v.max(out=m8.ap(), in_=st8.ap())
                    v.tensor_scalar_mul(out=st1.ap(), in0=m8.ap()[:, 0:1],
                                            scalar1=-1.0)
                V(_)

                def _(sc):
                    sc.wait_ge(sems["vt"], cnt["vt"])
                    sc.activation(out=t8.ap(), in_=st8.ap(), func=AF.Exp,
                                  bias=st1.ap(), accum_out=st2.ap())
                    sc.activation(out=st3.ap(), in_=m8.ap()[:, 1:2], func=AF.Exp,
                                      bias=st1.ap())
                SC(_)

                def _(v, i=i):
                    v.wait_ge(sems["sv"], cnt["sv"])
                    v.reciprocal(out=st2.ap(), in_=st2.ap())
                    v.tensor_scalar(out=t8.ap(), in0=t8.ap(), scalar1=st2.ap(),
                                    scalar2=None, op0=ALU.mult)
                    v.tensor_tensor(out=st3.ap(), in0=st3.ap(), in1=st2.ap(),
                                    op=ALU.mult)
                    v.tensor_scalar(out=m8.ap(), in0=t8.ap(), scalar1=st3.ap(),
                                    scalar2=None, op0=ALU.is_ge)
                    v.tensor_tensor(out=st5.ap(), in0=st2.ap(), in1=st3.ap(), op=ALU.add)
                    v.tensor_scalar(out=st5.ap(), in0=st5.ap(), scalar1=1e-8,
                                    scalar2=None, op0=ALU.add)
                    v.reciprocal(out=st5.ap(), in_=st5.ap())
                    v.tensor_tensor(out=t8.ap(), in0=t8.ap(), in1=m8.ap(), op=ALU.mult)
                    v.tensor_scalar(out=cw_c.ap()[:, i * 8:(i + 1) * 8], in0=t8.ap(),
                                        scalar1=st5.ap(), scalar2=None, op0=ALU.mult)
                V(_)
            barrier()
            if s == 0:
                tap_bf16("normed2", normed2, NCH * TSEQ)
                tap_f32("cw", cw_c, NT * E8)
            if STAGE[0] < 4:
                return

            for hh in range(2):
                T0 = hh * 1024
                for e in range(E8):
                    tmark = [0, 0]
                    for fo in range(16):
                        pair = fo % 2
                        w1s, w3s = wsl[pair * 2], wsl[pair * 2 + 1]

                        def _(g, e=e, fo=fo, pair=pair, w1s=w1s, w3s=w3s):
                            g.wait_ge(sems["ev"], tmark[pair])
                            load_lhsT(g, w1s, W1[e], fo * 128)
                            load_lhsT(g, w3s, W3[e], fo * 128)
                        G_(_)

                        def _(t, fo=fo, T0=T0, pair=pair, w1s=w1s, w3s=w3s):
                            wait_dma(t, ssem(w1s), ssem(w3s))
                            t.wait_ge(sems["vt"], cnt["vt"])
                            t.wait_ge(sems["sv"], cnt["sv"])
                            last = None
                            for tb in range(2):
                                for c in range(NCH):
                                    t.matmul(psq[tb],
                                             w1s.ap()[:, c * 128:(c + 1) * 128],
                                             fmv(normed2, c, T0 + tb * 512, 512),
                                             start=(c == 0), stop=(c == NCH - 1))
                                for c in range(NCH):
                                    last = t.matmul(
                                        psq[2 + tb], w3s.ap()[:, c * 128:(c + 1) * 128],
                                        fmv(normed2, c, T0 + tb * 512, 512),
                                        start=(c == 0), stop=(c == NCH - 1))
                            inc(last, "ev")
                            tmark[pair] = cnt["ev"]
                        T(_)

                        for tb in range(2):
                            def _(sc, tb=tb):
                                sc.wait_ge(sems["ev"], cnt["ev"])
                                sc.wait_ge(sems["vt"], cnt["vt"] - 1 if tb else cnt["vt"])
                                sc.activation(
                                    out=msl.ap()[:, tb * 512:(tb + 1) * 512],
                                    in_=psq[tb], func=AF.Silu)
                            SC(_)

                            def _(v, tb=tb, fo=fo):
                                v.wait_ge(sems["sv"], cnt["sv"])
                                v.tensor_tensor(
                                    out=hfm.ap()[:, fo * 1024 + tb * 512:
                                                 fo * 1024 + (tb + 1) * 512],
                                    in0=msl.ap()[:, tb * 512:(tb + 1) * 512],
                                    in1=psq[2 + tb], op=ALU.mult)
                            V(_)

                    for w2h in range(2):
                        def _(g, e=e, w2h=w2h):
                            g.wait_ge(sems["ev"], cnt["ev"])
                            inc(g.dma_start(
                                out=w2r.ap().rearrange("p (c n) -> p c n", n=1024),
                                in_=W2[e][w2h * 1024:(w2h + 1) * 1024, :]
                                .rearrange("(c p) n -> p c n", p=128)), "w2r", 16)
                        G_(_)

                        for tb in range(8):
                            ti = hh * 8 + tb
                            for dh in range(2):
                                pq = psq[dh]

                                def _(t, tb=tb, dh=dh, w2h=w2h, pq=pq):
                                    wait_dma(t, "w2r")
                                    if tb == 0 and dh == 0:
                                        t.wait_ge(sems["vt"], cnt["vt"])
                                    else:
                                        t.wait_ge(sems["vt"], cnt["vt"] - 1)
                                    last = None
                                    for j in range(8):
                                        hc = w2h * 8 + j
                                        last = t.matmul(
                                            pq,
                                            hfm.ap()[:, hc * 1024 + tb * 128:
                                                     hc * 1024 + tb * 128 + 128],
                                            w2r.ap()[:, j * 1024 + dh * 512:
                                                     j * 1024 + dh * 512 + 512],
                                            start=(j == 0), stop=(j == 7))
                                    inc(last, "ev")
                                T(_)

                                def _(v, ti=ti, dh=dh, e=e, w2h=w2h, pq=pq):
                                    v.wait_ge(sems["ev"], cnt["ev"])
                                    dst = moe_acc.ap()[:, ti * D + dh * 512:
                                                       ti * D + (dh + 1) * 512]
                                    cwcol = cw_c.ap()[:, ti * 8 + e: ti * 8 + e + 1]
                                    if e == 0 and w2h == 0:
                                        v.tensor_scalar(out=dst, in0=pq,
                                                            scalar1=cwcol, scalar2=None,
                                                            op0=ALU.mult)
                                    else:
                                        v.scalar_tensor_tensor(
                                            out=dst, in0=pq, scalar=cwcol, in1=dst,
                                            op0=ALU.mult, op1=ALU.add)
                                V(_)
            barrier()
            if s == 0:
                tap_f32("moe", moe_acc, NT * D)

            build_ls(LS2R)
            for i in range(NT):
                R0 = SOFF + i * 128

                def _(sync, i=i, R0=R0):
                    sync.wait_ge(sems["vt"], cnt["vt"])
                    sync.wait_ge(sems["sv"], cnt["sv"])
                    wait_dma(sync, "x1s")
                    inc(sync.dma_start(out=xd1.ap(), in_=X0[R0:R0 + 128, :]), "xdl", 16)
                    inc(sync.dma_start(out=xd2.ap(), in_=NIN_D[R0:R0 + 128, :]), "xdl", 16)
                    inc(sync.dma_start(out=xd3.ap(), in_=H2_D[i * 128:(i + 1) * 128, :]),
                        "xdl", 16)
                S(_)

                def _(v, i=i):
                    wait_dma(v, "xdl")
                    v.tensor_tensor(out=xd4.ap(), in0=moe_acc.ap()[:, i * D:(i + 1) * D],
                                    in1=lsb.ap(), op=ALU.mult)
                    v.tensor_tensor(out=xd4.ap(), in0=xd3.ap(), in1=xd4.ap(), op=ALU.add)
                    v.scalar_tensor_tensor(out=xd4.ap(), in0=xd4.ap(),
                                           scalar=ws_c.ap()[:, i:i + 1], in1=xd2.ap(),
                                           op0=ALU.mult, op1=ALU.add)
                    v.tensor_tensor(out=xd1.ap(), in0=xd1.ap(), in1=xd4.ap(),
                                        op=ALU.add)
                V(_)

                def _(sync, i=i):
                    sync.wait_ge(sems["vt"], cnt["vt"])
                    inc(sync.dma_start(out=X1_D[i * 128:(i + 1) * 128, :], in_=xd1.ap()),
                        "x1s", 16)
                S(_)

                ln_stats(xd1.ap(), xsq2.ap())

                def _(v):
                    v.wait_ge(sems["sv"], cnt["sv"])
                    v.tensor_scalar(out=ztd.ap(), in0=xd1.ap(), scalar1=st1.ap(),
                                        scalar2=st3.ap(), op0=ALU.subtract,
                                        op1=ALU.mult)
                V(_)
                transpose_to_fm(ztd.ap(), ffm, i * 128)
            barrier()
            if s == 0:
                tap_bf16("ffm", ffm, NCH * TSEQ)

        def emit_d_phases(s, SOFF):
            def _(g):
                inc(g.dma_start(out=ue_s.ap().rearrange("p (c n) -> p c n", n=1024),
                                in_=UED.rearrange("(c p) n -> p c n", p=128)), "ue", 16)
            G_(_)

            for qo in range(4):
                def _(g, qo=qo):
                    g.wait_ge(sems["ev"], cnt["ev"])
                    load_lhsT(g, wsl[qo % 2], WQT, qo * 128)
                G_(_)

                def _(t, qo=qo):
                    wait_dma(t, ["wsl0", "wsl1"][qo % 2])
                    t.wait_ge(sems["vt"], cnt["vt"])
                    last = None
                    for tb in range(4):
                        for c in range(NCH):
                            last = t.matmul(psq[tb],
                                            wsl[qo % 2].ap()[:, c * 128:(c + 1) * 128],
                                            fmv(ffm, c, tb * 512, 512),
                                            start=(c == 0), stop=(c == NCH - 1))
                    inc(last, "ev")
                T(_)

                def _(v, qo=qo):
                    v.wait_ge(sems["ev"], cnt["ev"])
                    last = None
                    for tb in range(4):
                        last = v.tensor_copy(out=qpk.ap()[:, qo * TSEQ + tb * 512:
                                                          qo * TSEQ + (tb + 1) * 512],
                                             in_=psq[tb])
                    pass
                V(_)

            for eo in range(8):
                def _(g, eo=eo):
                    g.wait_ge(sems["ev"], cnt["ev"] - 1 if eo else cnt["ev"])
                    inc(g.dma_start(
                        out=dsl[eo % 2].ap().rearrange("p (c n) -> p c n", n=512),
                        in_=DET[:, eo * 512:(eo + 1) * 512]
                        .rearrange("(c p) n -> p c n", p=128)), ["dsl0", "dsl1"][eo % 2], 16)
                G_(_)

                for tb in range(NT):
                    pq = psq[tb % 2]
                    gb = gsp[tb % 2]

                    def _(t, eo=eo, tb=tb, pq=pq):
                        wait_dma(t, ["dsl0", "dsl1"][eo % 2])
                        if tb < 2:
                            t.wait_ge(sems["vt"], cnt["vt"])
                        else:
                            t.wait_ge(sems["vt"], cnt["vt"] - 1)
                        last = None
                        for c in range(NCH):
                            last = t.matmul(pq, fmv(ffm, c, tb * 128, 128),
                                            dsl[eo % 2].ap()[:, c * 512:(c + 1) * 512],
                                            start=(c == 0), stop=(c == NCH - 1))
                        inc(last, "ev")
                    T(_)

                    def _(v, tb=tb, pq=pq, gb=gb):
                        v.wait_ge(sems["ev"], cnt["ev"])
                        wait_dma(v, ssem(gb))
                        v.tensor_copy(out=gb.ap(), in_=pq)
                    V(_)

                    def _(sync, eo=eo, tb=tb, gb=gb):
                        sync.wait_ge(sems["vt"], cnt["vt"])
                        inc(sync.dma_start(
                            out=GDD[tb * 128:(tb + 1) * 128, eo * 512:(eo + 1) * 512],
                            in_=gb.ap()), ssem(gb), 16)
                    S(_)
            barrier()
            if s == 0:
                tap_bf16("qpk", qpk, 4 * TSEQ)
            if STAGE[0] < 6:
                return

            SL = {"mx8": 0, "my8": 8, "ixf": 16, "iyp": 24, "cand": 32, "c8": 96,
                  "mask": 104, "eofc": 168, "em": 232, "e8p": 296, "pexp": 304,
                  "pm": 368, "EQ": 432, "PP": 944, "p8": 1456, "a8": 1464, "idxf": 1472}

            def slv(nm, w):
                o = SL[nm]
                return sel.ap()[:, o:o + w]

            for tb in range(NT):
                def _(t, tb=tb):
                    t.wait_ge(sems["vt"], cnt["vt"])
                    for p in range(2):
                        for hh in range(PH):
                            qo = p * 2 + hh // 2
                            rb2 = 64 * (hh % 2)
                            if p or hh:
                                t.wait_ge(sems["ev"], cnt["ev"])
                            inc(t.matmul(
                                ps_a.ap()[:, (p * 4 + hh) * 64:(p * 4 + hh) * 64 + 64],
                                qpk.ap()[rb2:rb2 + 64, qo * TSEQ + tb * 128:
                                         qo * TSEQ + tb * 128 + 128],
                                keys_s.ap()[rb2:rb2 + 64,
                                            (p * 4 + hh) * 64:(p * 4 + hh) * 64 + 64],
                                start=True, stop=True), "ev")
                T(_)

                def _(v, tb=tb):
                    v.wait_ge(sems["ev"], cnt["ev"])
                    v.tensor_copy(out=sims.ap(), in_=ps_a.ap())
                V(_)

                for hh in range(PH if D2LIM[0] >= 2 else 0):
                    sx = sims.ap()[:, hh * 64:(hh + 1) * 64]
                    sy = sims.ap()[:, (PH + hh) * 64:(PH + hh + 1) * 64]

                    def _(v, tb=tb, hh=hh, sx=sx, sy=sy):
                        v.wait_ge(sems["go"], cnt["go"])
                        v.max(out=slv("mx8", 8), in_=sx)
                        v.max_index(out=selu.ap()[:, 0:8], in_max=slv("mx8", 8),
                                    in_values=sx)
                        v.max(out=slv("my8", 8), in_=sy)
                        v.max_index(out=selu.ap()[:, 8:16], in_max=slv("my8", 8),
                                    in_values=sy)
                        v.tensor_copy(out=slv("ixf", 8), in_=selu.ap()[:, 0:8])
                        v.tensor_copy(out=slv("iyp", 8), in_=selu.ap()[:, 8:16])
                        v.tensor_scalar_add(out=slv("iyp", 8), in0=slv("iyp", 8),
                                            scalar1=1.0)
                        rep = slv("mx8", 8).rearrange("p (r o) -> p r o", o=1) \
                            .to_broadcast([128, 8, 8])
                        til = slv("my8", 8).rearrange("p (o r) -> p o r", o=1) \
                            .to_broadcast([128, 8, 8])
                        v.tensor_tensor(
                            out=slv("cand", 64).rearrange("p (r c) -> p r c", c=8),
                            in0=rep, in1=til, op=ALU.add)
                        v.max(out=slv("c8", 8), in_=slv("cand", 64))
                        v.tensor_scalar(out=slv("mask", 64), in0=slv("cand", 64),
                                        scalar1=slv("c8", 8)[:, 7:8], scalar2=None,
                                        op0=ALU.is_ge)
                        repx = slv("ixf", 8).rearrange("p (r o) -> p r o", o=1) \
                            .to_broadcast([128, 8, 8])
                        tily = slv("iyp", 8).rearrange("p (o r) -> p o r", o=1) \
                            .to_broadcast([128, 8, 8])
                        v.scalar_tensor_tensor(
                            out=slv("eofc", 64).rearrange("p (r c) -> p r c", c=8),
                            in0=repx, scalar=64.0, in1=tily, op0=ALU.mult, op1=ALU.add)
                        v.tensor_tensor(out=slv("em", 64), in0=slv("eofc", 64),
                                        in1=slv("mask", 64), op=ALU.mult)
                        v.max(out=slv("e8p", 8), in_=slv("em", 64))
                        v.tensor_scalar_mul(out=st1.ap(), in0=slv("c8", 8)[:, 0:1],
                                                scalar1=-1.0)
                    V(_)

                    def _(sc):
                        sc.wait_ge(sems["vt"], cnt["vt"])
                        sc.activation(out=slv("pexp", 64), in_=slv("cand", 64),
                                          func=AF.Exp, bias=st1.ap())
                    SC(_)

                    def _(v, tb=tb, hh=hh):
                        v.wait_ge(sems["sv"], cnt["sv"])
                        v.tensor_tensor(out=slv("pm", 64), in0=slv("pexp", 64),
                                        in1=slv("mask", 64), op=ALU.mult)
                        v.reduce_sum(out=st2.ap(), in_=slv("pm", 64), axis=AX.X)
                        v.reciprocal(out=st2.ap(), in_=st2.ap())
                        v.tensor_scalar(out=slv("pm", 64), in0=slv("pm", 64),
                                        scalar1=st2.ap(), scalar2=None, op0=ALU.mult)
                        repe = slv("e8p", 8).rearrange("p (j o) -> p j o", o=1) \
                            .to_broadcast([128, 8, 64])
                        bce = slv("eofc", 64).rearrange("p (o c) -> p o c", o=1) \
                            .to_broadcast([128, 8, 64])
                        v.tensor_tensor(
                            out=slv("EQ", 512).rearrange("p (j c) -> p j c", c=64),
                            in0=repe, in1=bce, op=ALU.is_equal)
                        bcp = slv("pm", 64).rearrange("p (o c) -> p o c", o=1) \
                            .to_broadcast([128, 8, 64])
                        v.tensor_tensor(
                            out=slv("PP", 512).rearrange("p (j c) -> p j c", c=64),
                            in0=slv("EQ", 512).rearrange("p (j c) -> p j c", c=64),
                            in1=bcp, op=ALU.mult)
                        v.reduce_sum(out=slv("p8", 8),
                                     in_=slv("PP", 512).rearrange("p (j c) -> p j c", c=64),
                                     axis=AX.X)
                        v.tensor_scalar_add(out=slv("a8", 8), in0=slv("e8p", 8),
                                            scalar1=-1.0)
                        iob = iotatl.ap().to_broadcast([128, 8])
                        v.scalar_tensor_tensor(out=slv("idxf", 8), in0=iob, scalar=4096.0,
                                               in1=slv("a8", 8), op0=ALU.mult, op1=ALU.add)
                        v.tensor_copy(out=seli.ap()[:, hh * 8:(hh + 1) * 8],
                                          in_=slv("idxf", 8))
                    V(_)

                    def _(g, tb=tb, hh=hh):
                        if D2LIM[0] < 3:
                            return
                        g.wait_ge(sems["vt"], cnt["vt"])
                        g.wait_ge(sems["go"], cnt["go"])
                        for j in range(8):
                            inc(g.indirect_dma_start(
                                out=selg.ap()[:, hh * 8 + j: hh * 8 + j + 1],
                                out_offset=None,
                                in_=GD_flat,
                                in_offset=IndirectOffsetOnAxis(
                                    seli.ap()[:, hh * 8 + j: hh * 8 + j + 1], 0),
                                element_offset=tb * 128 * NEXP), "go", 16)
                    G_(_)

                    def _(v, tb=tb, hh=hh):
                        if D2LIM[0] < 3:
                            return
                        v.wait_ge(sems["go"], cnt["go"])
                        v.tensor_tensor(out=t8.ap(),
                                            in0=selg.ap()[:, hh * 8:(hh + 1) * 8],
                                            in1=slv("p8", 8), op=ALU.mult)
                    V(_)

                    def _(sc, hh=hh):
                        if D2LIM[0] < 3:
                            return
                        sc.wait_ge(sems["vt"], cnt["vt"])
                        sc.activation(out=selw.ap()[:, hh * 8:(hh + 1) * 8],
                                          in_=t8.ap(), func=AF.Silu)
                    SC(_)

                    def _(g, tb=tb, hh=hh, SOFF=SOFF):
                        if D2LIM[0] < 3:
                            return
                        g.wait_ge(sems["sv"], cnt["sv"])
                        wait_dma(g, "rdz", "gsc")
                        for j in range(8):
                            inc(g.indirect_dma_start(
                                out=RD_flat,
                                out_offset=IndirectOffsetOnAxis(
                                    seli.ap()[:, hh * 8 + j: hh * 8 + j + 1], 0),
                                in_=selw.ap()[:, hh * 8 + j: hh * 8 + j + 1],
                                in_offset=None,
                                element_offset=(SOFF + tb * 128) * NEXP,
                                compute_op=ALU.add), "gsc", 16)
                    G_(_)

                if s == 0 and "sele" in tap_out:
                    def _(v, tb=tb):
                        v.wait_ge(sems["go"], cnt["go"])
                        wait_dma(v, "gsc", "tap")
                        v.tensor_copy(out=lsb.ap()[:, 0:32], in_=seli.ap())
                        v.tensor_copy(out=lsb.ap()[:, 32:64], in_=selw.ap())
                    V(_)

                    def _(sync, tb=tb):
                        sync.wait_ge(sems["vt"], cnt["vt"])
                        inc(sync.dma_start(out=tap_out["sele"][:, tb * 32:(tb + 1) * 32],
                                           in_=lsb.ap()[:, 0:32]), "tap", 16)
                        inc(sync.dma_start(out=tap_out["selwt"][:, tb * 32:(tb + 1) * 32],
                                           in_=lsb.ap()[:, 32:64]), "tap", 16)
                    S(_)
            barrier()
            if STAGE[0] < 7:
                return

            for tg in range(4):
                TG0 = tg * 512
                for io in range(32):
                    def _(g, io=io):
                        g.wait_ge(sems["ev"], cnt["ev"] - 1 if io else cnt["ev"])
                        load_lhsT(g, wsl[io % 2], WUPT, io * 128)
                    G_(_)

                    def _(t, io=io, TG0=TG0):
                        wait_dma(t, ["wsl0", "wsl1"][io % 2])
                        t.wait_ge(sems["sv"], cnt["sv"] - 1 if io else cnt["sv"])
                        pq = [ps_a, ps_b][io % 2]
                        last = None
                        for c in range(NCH):
                            last = t.matmul(pq.ap(),
                                            wsl[io % 2].ap()[:, c * 128:(c + 1) * 128],
                                            fmv(ffm, c, TG0, 512),
                                            start=(c == 0), stop=(c == NCH - 1))
                        inc(last, "ev")
                    T(_)

                    def _(sc, io=io):
                        sc.wait_ge(sems["ev"], cnt["ev"])
                        pq = [ps_a, ps_b][io % 2]
                        sc.activation(out=hfd.ap()[:, io * 512:(io + 1) * 512],
                                          in_=pq.ap(), func=AF.Silu)
                    SC(_)

                for dh in range(2):
                    def _(g, dh=dh):
                        g.wait_ge(sems["ev"], cnt["ev"])
                        inc(g.dma_start(
                            out=wdnr.ap().rearrange("p (c n) -> p c n", n=512),
                            in_=WDNT[:, dh * 512:(dh + 1) * 512]
                            .rearrange("(c p) n -> p c n", p=128)), "wdnr", 16)
                    G_(_)

                    for tbl in range(4):
                        tb = tg * 4 + tbl
                        RT0 = SOFF + tb * 128

                        def _(g, RT0=RT0):
                            g.wait_ge(sems["vt"], cnt["vt"])
                            wait_dma(g, "gsc")
                            inc(g.dma_start(out=rlb.ap(), in_=RDD[RT0:RT0 + 128, :]),
                                "rlb", 16)
                        G_(_)

                        for ec in range(32):
                            pst = [ps_t0, ps_t1][ec % 2]

                            def _(t, ec=ec, pst=pst):
                                wait_dma(t, "rlb")
                                if ec < 2:
                                    t.wait_ge(sems["vt"], cnt["vt"])
                                else:
                                    t.wait_ge(sems["vt"], cnt["vt"] - 1)
                                inc(t.transpose(pst.ap()[:, 0:128],
                                                rlb.ap()[:, ec * 128:(ec + 1) * 128],
                                                identb.ap()), "ev")
                            T(_)

                            def _(v, ec=ec, pst=pst):
                                v.wait_ge(sems["ev"], cnt["ev"])
                                v.tensor_copy(out=rtt.ap()[:, ec * 128:(ec + 1) * 128],
                                                  in_=pst.ap()[:, 0:128])
                            V(_)

                        def _(sync, tb=tb, dh=dh):
                            sync.wait_ge(sems["vt"], cnt["vt"])
                            wait_dma(sync, "out")
                            inc(sync.dma_start(out=xe1.ap()[:, 0:512],
                                               in_=X1_D[tb * 128:(tb + 1) * 128,
                                                        dh * 512:(dh + 1) * 512]),
                                "xe1", 16)
                        S(_)

                        def _(t, tbl=tbl, dh=dh):
                            t.wait_ge(sems["vt"], cnt["vt"])
                            t.wait_ge(sems["sv"], cnt["sv"])
                            wait_dma(t, "wdnr", "ue")
                            pq = psq[tbl % 2]
                            for ic in range(32):
                                t.matmul(pq, hfd.ap()[:, ic * 512 + tbl * 128:
                                                      ic * 512 + tbl * 128 + 128],
                                         wdnr.ap()[:, ic * 512:(ic + 1) * 512],
                                         start=(ic == 0), stop=False)
                            last = None
                            for ec in range(32):
                                last = t.matmul(pq, rtt.ap()[:, ec * 128:(ec + 1) * 128],
                                                ue_s.ap()[:, ec * 1024 + dh * 512:
                                                          ec * 1024 + dh * 512 + 512],
                                                start=False, stop=(ec == 31))
                            inc(last, "ev")
                        T(_)

                        def _(v, tbl=tbl):
                            v.wait_ge(sems["ev"], cnt["ev"])
                            wait_dma(v, "xe1", "out")
                            pq = psq[tbl % 2]
                            v.tensor_tensor(out=xe2.ap()[:, 0:512], in0=pq,
                                                in1=xe1.ap()[:, 0:512], op=ALU.add)
                        V(_)

                        def _(sync, RT0=RT0, dh=dh):
                            sync.wait_ge(sems["vt"], cnt["vt"])
                            inc(sync.dma_start(out=OUT[RT0:RT0 + 128,
                                                       dh * 512:(dh + 1) * 512],
                                               in_=xe2.ap()[:, 0:512]), "out", 16)
                        S(_)
            barrier()

        for s in range(min(NSEQ, SEQLIM[0])):
            SOFF = s * TSEQ
            if STAGE[0] < 1:
                break

            for i in range(NT):
                R0 = SOFF + i * 128

                def _(sync, R0=R0):
                    sync.wait_ge(sems["vt"], cnt["vt"])
                    inc(sync.dma_start(out=xa.ap(), in_=X0[R0:R0 + 128, :]), "xa", 16)
                S(_)
                ln_stats(xa.ap(), xsq.ap(), pre="xa")

                def _(v):
                    v.wait_ge(sems["sv"], cnt["sv"])
                    wait_dma(v, "nsp")
                    v.tensor_scalar(out=xb.ap(), in0=xa.ap(), scalar1=st1.ap(),
                                        scalar2=st3.ap(), op0=ALU.subtract,
                                        op1=ALU.mult)
                V(_)

                def _(sync, R0=R0):
                    sync.wait_ge(sems["vt"], cnt["vt"])
                    inc(sync.dma_start(out=NIN_D[R0:R0 + 128, :], in_=xb.ap()), "nsp", 16)
                S(_)
                ln_stats(xb.ap(), xsq.ap())

                def _(v):
                    v.wait_ge(sems["sv"], cnt["sv"])
                    v.tensor_scalar(out=zt.ap(), in0=xb.ap(), scalar1=st1.ap(),
                                        scalar2=st3.ap(), op0=ALU.subtract,
                                        op1=ALU.mult)
                V(_)
                transpose_to_fm(zt.ap(), normed, i * 128, g1c, b1c)
            barrier()
            if s == 0:
                tap_bf16("normed", normed, NCH * TSEQ)
            if STAGE[0] < 2:
                continue

            def _(v):
                v.memset(vtm.ap(), 1.0)
            V(_)

            for fo in range(16):
                slot = wsl[fo % 2]

                def _(g, slot=slot):
                    g.wait_ge(sems["ev"], max(0, cnt["ev"] - 1))
                    load_lhsT(g, slot, WQKV, fo * 128)
                G_(_)

                for tb in range(4):
                    def _(t, slot=slot, tb=tb):
                        wait_dma(t, ssem(slot))
                        if tb < 2:
                            t.wait_ge(sems["vt"], cnt["vt"])
                        else:
                            t.wait_ge(sems["vt"], cnt["vt"] - 1)
                        last = None
                        for c in range(NCH):
                            last = t.matmul([ps_a, ps_b][tb % 2].ap(),
                                            slot.ap()[:, c * 128:(c + 1) * 128],
                                            fmv(normed, c, tb * 512, 512),
                                            start=(c == 0), stop=(c == NCH - 1))
                        inc(last, "ev")
                    T(_)

                    def _(v, fo=fo, tb=tb):
                        dstbuf = qf if fo < 8 else kf
                        c_out = fo % 8
                        scale = 0.125 if fo < 8 else 1.0
                        v.wait_ge(sems["ev"], cnt["ev"])
                        v.tensor_scalar_mul(out=fmv(dstbuf, c_out, tb * 512, 512),
                                                in0=[ps_a, ps_b][tb % 2].ap(),
                                                scalar1=scale)
                    V(_)
            barrier()
            if s == 0:
                tap_bf16("qf", qf, NCH * TSEQ)
                tap_bf16("kf", kf, NCH * TSEQ)

            def _(g):
                load_rhs_blk(g, wv0, WQKV, 2048)
                load_rhs_blk(g, wv1, WQKV, 2048 + 512)
            G_(_)

            for i in range(NT):
                for vh in range(2):
                    slot = [wv0, wv1][vh]

                    def _(t, i=i, vh=vh, slot=slot):
                        wait_dma(t, ssem(slot))
                        t.wait_ge(sems["sv"], cnt["sv"])
                        t.wait_ge(sems["vt"], cnt["vt"] - 1 if (i or vh) else cnt["vt"])
                        last = None
                        for c in range(NCH):
                            last = t.matmul([ps_a, ps_b][vh].ap(),
                                            fmv(normed, c, i * 128, 128),
                                            slot.ap()[:, c * 512:(c + 1) * 512],
                                            start=(c == 0), stop=(c == NCH - 1))
                        inc(last, "ev")
                    T(_)

                def _(v, i=i):
                    v.wait_ge(sems["ev"], cnt["ev"])
                    base = i * NH * 65
                    last = None
                    for vh in range(2):
                        pq = [ps_a, ps_b][vh]
                        dst = vtm.ap()[:, base + vh * 8 * 65: base + (vh * 8 + 8) * 65] \
                            .rearrange("p (h j) -> p h j", j=65)[:, :, 0:64]
                        last = v.tensor_copy(out=dst,
                                             in_=pq.ap().rearrange("p (h j) -> p h j",
                                                                   j=64))
                    pass
                V(_)

                def _(t, i=i):
                    t.wait_ge(sems["vt"], cnt["vt"])
                    for c in range(NCH):
                        t.matmul(ps_a.ap()[:, 0:2], fmv(normed, c, i * 128, 128),
                                 wr_s.ap()[:, c * 2:(c + 1) * 2],
                                 start=(c == 0), stop=False)
                    inc(t.matmul(ps_a.ap()[:, 0:2], ones_b.ap()[0:1, 0:128], brr.ap(),
                                 start=False, stop=True), "ev")
                T(_)

                def _(sc, i=i):
                    sc.wait_ge(sems["ev"], cnt["ev"])
                    sc.activation(out=al_c.ap()[:, i:i + 1], in_=ps_a.ap()[:, 0:1],
                                  func=AF.Sigmoid)
                    sc.activation(out=st1.ap(), in_=ps_a.ap()[:, 1:2], func=AF.Sigmoid,
                                  bias=sbb.ap()[:, 0:1])
                    sc.activation(out=st2.ap(), in_=ps_a.ap()[:, 1:2], func=AF.Sigmoid,
                                  bias=sbb.ap()[:, 1:2])
                    sc.activation(out=st3.ap(), in_=st1.ap(), func=AF.Relu,
                                      scale=m1c.ap(), bias=onec.ap())
                SC(_)

                def _(v, i=i):
                    v.wait_ge(sems["sv"], cnt["sv"])
                    v.tensor_tensor(out=st2.ap(), in0=st2.ap(), in1=st3.ap(), op=ALU.mult)
                    v.scalar_tensor_tensor(out=st5.ap(), in0=st2.ap(), scalar=-1.0,
                                           in1=st3.ap(), op0=ALU.mult, op1=ALU.add)
                    v.tensor_relu(out=st5.ap(), in_=st5.ap())
                    v.tensor_add(out=st1.ap(), in0=st1.ap(), in1=st2.ap())
                    v.tensor_add(out=ws_c.ap()[:, i:i + 1], in0=st1.ap(),
                                     in1=st5.ap())
                V(_)
            barrier()
            if s == 0:
                tap_f32("ws", ws_c, NT)
                tap_f32("alpha", al_c, NT)

            for h in range(NH):
                ch, rb = h // 2, 64 * (h % 2)
                for qt in range(NT):
                    def _(t, qt=qt, ch=ch, rb=rb):
                        t.wait_ge(sems["vt"], cnt["vt"])
                        last = None
                        for kb in range(4):
                            last = t.matmul(
                                psq[kb],
                                qf.ap()[rb:rb + 64, ch * TSEQ + qt * 128:
                                        ch * TSEQ + qt * 128 + 128],
                                kf.ap()[rb:rb + 64, ch * TSEQ + kb * 512:
                                        ch * TSEQ + (kb + 1) * 512],
                                start=True, stop=True)
                        inc(last, "ev")
                    T(_)

                    def _(v, qt=qt):
                        v.wait_ge(sems["ev"], cnt["ev"])
                        v.reduce_max(out=st4w.ap(),
                                     in_=ps_big.ap().rearrange("p (b n) -> p b n", n=512),
                                     axis=AX.X)
                        v.reduce_max(out=st1.ap(), in_=st4w.ap(), axis=AX.X)
                        v.tensor_scalar_mul(out=st1.ap(), in0=st1.ap(),
                                                scalar1=-1.0)
                    V(_)

                    def _(t, qt=qt):
                        t.wait_ge(sems["vt"], cnt["vt"])
                        inc(t.transpose(ps_a.ap()[0:1, 0:128], st1.ap(), identf.ap()),
                            "ev")
                    T(_)

                    def _(v, qt=qt):
                        v.wait_ge(sems["ev"], cnt["ev"])
                        v.tensor_copy(out=nmrow.ap()[0:1, qt * 128:(qt + 1) * 128],
                                          in_=ps_a.ap()[0:1, 0:128])
                    V(_)

                for qg in range(16):
                    q0 = qg * 128
                    for kc in range(16):
                        pq = [ps_a, ps_b][kc % 2]

                        def _(t, kc=kc, q0=q0, ch=ch, rb=rb, pq=pq):
                            if kc < 2:
                                t.wait_ge(sems["vt"], cnt["vt"])
                                t.wait_ge(sems["sv"], cnt["sv"])
                            else:
                                t.wait_ge(sems["sv"], cnt["sv"] - 1)
                            t.matmul(pq.ap()[:, 0:128],
                                     kf.ap()[rb:rb + 64, ch * TSEQ + kc * 128:
                                             ch * TSEQ + kc * 128 + 128],
                                     qf.ap()[rb:rb + 64, ch * TSEQ + q0:
                                             ch * TSEQ + q0 + 128],
                                     start=True, stop=False)
                            inc(t.matmul(pq.ap()[:, 0:128], ones_b.ap()[0:1, 0:128],
                                         nmrow.ap()[0:1, q0:q0 + 128],
                                         start=False, stop=True), "ev")
                        T(_)

                        def _(sc, kc=kc, pq=pq):
                            sc.wait_ge(sems["ev"], cnt["ev"])
                            sc.activation(out=expT.ap()[:, kc * 128:(kc + 1) * 128],
                                              in_=pq.ap()[:, 0:128], func=AF.Exp)
                        SC(_)

                    def _(t, h=h):
                        t.wait_ge(sems["sv"], cnt["sv"])
                        t.wait_ge(sems["vt"], cnt["vt"])
                        last = None
                        for kc in range(16):
                            last = t.matmul(
                                ps_big.ap()[0:65, 1024:1152],
                                vtm.ap()[:, kc * NH * 65 + h * 65:
                                         kc * NH * 65 + h * 65 + 65],
                                expT.ap()[:, kc * 128:(kc + 1) * 128],
                                start=(kc == 0), stop=(kc == 15))
                        inc(last, "ev")
                    T(_)

                    def _(v):
                        v.wait_ge(sems["ev"], cnt["ev"])
                        v.reciprocal(out=zrow.ap()[0:1, 0:128],
                                         in_=ps_big.ap()[64:65, 1024:1152])
                    V(_)

                    def _(t):
                        t.wait_ge(sems["vt"], cnt["vt"])
                        inc(t.matmul(ps_big.ap()[0:64, 1536:1664], ones_f.ap()[0:1, 0:64],
                                     zrow.ap()[0:1, 0:128], start=True, stop=True), "ev")
                    T(_)

                    def _(v, ch=ch, rb=rb, q0=q0):
                        v.wait_ge(sems["ev"], cnt["ev"])
                        v.tensor_copy(out=zbb.ap()[0:64, 0:128],
                                      in_=ps_big.ap()[0:64, 1536:1664])
                        v.tensor_tensor(
                            out=attnO.ap()[rb:rb + 64, ch * TSEQ + q0:
                                           ch * TSEQ + q0 + 128],
                            in0=ps_big.ap()[0:64, 1024:1152], in1=zbb.ap()[0:64, 0:128],
                            op=ALU.mult)
                    V(_)
            barrier()
            if s == 0:
                tap_bf16("attnO", attnO, NCH * TSEQ)
            if STAGE[0] < 3:
                continue

            emit_c_phases(s, SOFF)
            if STAGE[0] < 5:
                continue
            emit_d_phases(s, SOFF)

        barrier()
    return nc

_BF = ml_dtypes.bfloat16


def _prep_inputs(inputs):
    f32 = np.float32
    g = lambda k: np.asarray(inputs[k], dtype=f32)

    x = g("L0_attn_x").reshape(32 * 2048, 1024)
    com = {}
    com["wr"] = np.ascontiguousarray(g("L0_attn_Wr"))
    com["brrow"] = np.ascontiguousarray(g("L0_attn_br").reshape(1, 2))
    com["sbrow"] = np.ascontiguousarray(g("L0_attn_step_bias").reshape(1, 3))
    for nm, key in [("g1c", "L0_attn_ln1_g"), ("b1c", "L0_attn_ln1_b"),
                    ("g2c", "L0_attn_ln2_g"), ("b2c", "L0_attn_ln2_b"),
                    ("cbc", "L0_attn_conv_b")]:
        com[nm] = np.ascontiguousarray(g(key).reshape(NCH, 128).T)
    com["ls1r"] = np.ascontiguousarray(g("L0_attn_ls1").reshape(1, D))
    com["ls2r"] = np.ascontiguousarray(g("L0_attn_ls2").reshape(1, D))
    cw = g("L0_attn_conv_w").reshape(D, 4)
    com["cwt"] = np.ascontiguousarray(cw.reshape(NCH, 128, 4).transpose(1, 0, 2)
                                      .reshape(128, NCH * 4))
    com["wqkv"] = np.ascontiguousarray(g("L0_attn_Wqkv"))
    com["wout"] = np.ascontiguousarray(g("L0_attn_Wout"))
    com["wgate"] = np.ascontiguousarray(g("L0_attn_Wgate"))
    com["win"] = np.ascontiguousarray(g("L0_attn_Win_ssm"))
    com["wossm"] = np.ascontiguousarray(g("L0_attn_Wout_ssm"))
    com["wg8"] = np.ascontiguousarray(g("L0_attn_Wg"))
    com["w1"] = np.ascontiguousarray(g("L0_attn_W1"))
    com["w3"] = np.ascontiguousarray(g("L0_attn_W3"))
    com["w2"] = np.ascontiguousarray(g("L0_attn_W2"))
    com["wupT"] = np.ascontiguousarray(g("L0_ffn_W_up").T)
    com["wdnT"] = np.ascontiguousarray(g("L0_ffn_W_down").T)
    com["wqT"] = np.ascontiguousarray(g("L0_ffn_W_q").T)
    com["deT"] = np.ascontiguousarray(g("L0_ffn_down_embed").T)
    com["ue"] = np.ascontiguousarray(g("L0_ffn_up_embed"))
    keys = g("L0_ffn_keys")
    kq = keys.transpose(2, 0, 3, 1).reshape(2 * PH, NKEY, NKEY)
    kq = kq.transpose(1, 0, 2).reshape(NKEY, 2 * PH * NKEY)
    kqd = np.concatenate([kq, kq], axis=0)
    com["keysq"] = np.ascontiguousarray(kqd)
    com["identb"] = np.eye(128, dtype=_BF)
    com["identf"] = np.eye(128, dtype=f32)
    com["iotatl"] = np.arange(128, dtype=f32).reshape(128, 1)
    com["iota64"] = np.tile(np.arange(64, dtype=f32), (128, 1))
    in_maps = []
    for c in range(8):
        m = dict(com)
        m["x0"] = np.ascontiguousarray(x[c * TCORE:(c + 1) * TCORE])
        in_maps.append(m)
    return in_maps


def kernel(**inputs):
    nc = build_nc(taps=DEBUG_TAPS)
    in_maps = _prep_inputs(inputs)
    res = run_bass_kernel_spmd(nc, in_maps, core_ids=list(range(8)))
    if DEBUG_TAPS:
        kernel.last_results = res.results
    out = np.concatenate([r["out"] for r in res.results], axis=0)
    return out.reshape(32, 2048, 1024)


# version 03 (final)
# speedup vs baseline: 1.1506x; 1.1506x over previous
import numpy as np
import ml_dtypes
from contextlib import ExitStack

import concourse.bass as bass
import concourse.mybir as mybir
from concourse.bass import IndirectOffsetOnAxis
from concourse.bass_utils import run_bass_kernel_spmd

F32 = mybir.dt.float32
BF16 = mybir.dt.bfloat16
I32 = mybir.dt.int32
U16 = mybir.dt.uint16
AF = mybir.ActivationFunctionType
ALU = mybir.AluOpType
AX = mybir.AxisListType

NSEQ, TSEQ, D = 4, 2048, 1024
NT = TSEQ // 128
NCH = D // 128
NH, DH = 16, 64
E8, FMOE = 8, 2048
IF, NEXP, PH, NKEY = 4096, 4096, 4, 64
TCORE = NSEQ * TSEQ
EPS = 1e-5
INV_D = 1.0 / D

DEBUG_TAPS = []
STAGE = [99]
SEQLIM = [4]
D2LIM = [3]

(PH_P0, PH_A, PH_B, PH_C1A, PH_C1B, PH_C2, PH_C3, PH_D1, PH_D2, PH_D3) = range(10)
P9 = PH_D3


class SbufPlan:

    def __init__(self, base=18432, limit=228352):
        self.base, self.limit, self.allocs = base, limit, []

    def place(self, name, size, ph0, ph1, align=64):
        size = (size + align - 1) // align * align
        taken = sorted((o, s) for (o, s, p0, p1, _n) in self.allocs
                       if not (p1 < ph0 or p0 > ph1))
        off = self.base
        for (o, s) in taken:
            if off + size <= o:
                break
            off = max(off, o + s)
        assert off + size <= self.limit, f"sbuf overflow at {name}: {off + size}"
        self.allocs.append((off, size, ph0, ph1, name))
        return off


def build_nc(taps=()):
    taps = list(taps)
    nc = bass.Bass("TRN2", target_bir_lowering=False)

    X0 = nc.dram_tensor("x0", [TCORE, D], F32, kind="ExternalInput").ap()
    WR = nc.dram_tensor("wr", [D, 2], F32, kind="ExternalInput").ap()
    BRROW = nc.dram_tensor("brrow", [1, 2], F32, kind="ExternalInput").ap()
    SBROW = nc.dram_tensor("sbrow", [1, 3], F32, kind="ExternalInput").ap()
    G1C = nc.dram_tensor("g1c", [128, NCH], F32, kind="ExternalInput").ap()
    B1C = nc.dram_tensor("b1c", [128, NCH], F32, kind="ExternalInput").ap()
    G2C = nc.dram_tensor("g2c", [128, NCH], F32, kind="ExternalInput").ap()
    B2C = nc.dram_tensor("b2c", [128, NCH], F32, kind="ExternalInput").ap()
    LS1R = nc.dram_tensor("ls1r", [1, D], F32, kind="ExternalInput").ap()
    LS2R = nc.dram_tensor("ls2r", [1, D], F32, kind="ExternalInput").ap()
    CWT = nc.dram_tensor("cwt", [128, NCH * 4], F32, kind="ExternalInput").ap()
    CBC = nc.dram_tensor("cbc", [128, NCH], F32, kind="ExternalInput").ap()
    WQKV = nc.dram_tensor("wqkv", [D, 3 * D], F32, kind="ExternalInput").ap()
    WOUT = nc.dram_tensor("wout", [D, D], F32, kind="ExternalInput").ap()
    WGATE = nc.dram_tensor("wgate", [D, D], F32, kind="ExternalInput").ap()
    WIN = nc.dram_tensor("win", [D, 2 * D], F32, kind="ExternalInput").ap()
    WOS = nc.dram_tensor("wossm", [D, D], F32, kind="ExternalInput").ap()
    WG8 = nc.dram_tensor("wg8", [D, E8], F32, kind="ExternalInput").ap()
    W1 = nc.dram_tensor("w1", [E8, D, FMOE], F32, kind="ExternalInput").ap()
    W3 = nc.dram_tensor("w3", [E8, D, FMOE], F32, kind="ExternalInput").ap()
    W2 = nc.dram_tensor("w2", [E8, FMOE, D], F32, kind="ExternalInput").ap()
    WUPT = nc.dram_tensor("wupT", [D, IF], F32, kind="ExternalInput").ap()
    WDNT = nc.dram_tensor("wdnT", [IF, D], F32, kind="ExternalInput").ap()
    WQT = nc.dram_tensor("wqT", [D, 512], F32, kind="ExternalInput").ap()
    DET = nc.dram_tensor("deT", [D, NEXP], F32, kind="ExternalInput").ap()
    UED = nc.dram_tensor("ue", [NEXP, D], F32, kind="ExternalInput").ap()
    KEYSQ = nc.dram_tensor("keysq", [128, 2 * PH * NKEY], F32, kind="ExternalInput").ap()
    IDENTB = nc.dram_tensor("identb", [128, 128], BF16, kind="ExternalInput").ap()
    IDENTF = nc.dram_tensor("identf", [128, 128], F32, kind="ExternalInput").ap()
    IOTATL = nc.dram_tensor("iotatl", [128, 1], F32, kind="ExternalInput").ap()
    IOTA64 = nc.dram_tensor("iota64", [128, 64], F32, kind="ExternalInput").ap()

    OUT = nc.dram_tensor("out", [TCORE, D], F32, kind="ExternalOutput").ap()

    NIN_D = nc.dram_tensor("nin_d", [TCORE, D], F32, kind="Internal").ap()
    H2_D = nc.dram_tensor("h2_d", [TSEQ, D], F32, kind="Internal").ap()
    X1_D = nc.dram_tensor("x1_d", [TSEQ, D], F32, kind="Internal").ap()
    GDD = nc.dram_tensor("gd", [TSEQ, NEXP], F32, kind="Internal").ap()
    RDD = nc.dram_tensor("rd", [TCORE, NEXP], F32, kind="Internal").ap()
    GD_flat = GDD.rearrange("a (b o) -> (a b) o", o=1)
    RD_flat = RDD.rearrange("a (b o) -> (a b) o", o=1)

    tap_shapes = {
        "normed": [128, NCH * TSEQ], "qf": [128, NCH * TSEQ], "kf": [128, NCH * TSEQ],
        "vtm": [128, NT * NH * 65], "attnO": [128, NCH * TSEQ], "u": [128, NCH * TSEQ],
        "normed2": [128, NCH * TSEQ], "cw": [128, NT * E8], "ws": [128, NT],
        "alpha": [128, NT], "ffm": [128, NCH * TSEQ], "qpk": [128, 4 * TSEQ],
        "moe": [128, NT * D], "sele": [128, NT * PH * 8], "selwt": [128, NT * PH * 8],
    }
    tap_out = {t: nc.dram_tensor("tap_" + t, tap_shapes[t], F32, kind="ExternalOutput").ap()
               for t in taps}

    plan = SbufPlan()

    sb_off = {}

    def mksb(name, shape, dtype, ph0, ph1):
        nb = {F32: 4, BF16: 2, I32: 4, U16: 2}[dtype]
        off = plan.place(name, int(np.prod(shape[1:])) * nb, ph0, ph1)
        sb_off[name] = off
        return nc.alloc_sbuf_tensor_at(name, list(shape), dtype, offset=off)

    def mksb_alias(name, shape, dtype, host, extra_off=0):
        return nc.alloc_sbuf_tensor_at(name, list(shape), dtype,
                                       offset=sb_off[host] + extra_off)

    g1c = mksb("g1c_s", [128, NCH], F32, 0, P9)
    b1c = mksb("b1c_s", [128, NCH], F32, 0, P9)
    g2c = mksb("g2c_s", [128, NCH], F32, 0, P9)
    b2c = mksb("b2c_s", [128, NCH], F32, 0, P9)
    cbc = mksb("cbc_s", [128, NCH], F32, 0, P9)
    cwt = mksb("cwt_s", [128, NCH * 4], F32, 0, P9)
    identb = mksb("identb_s", [128, 128], BF16, 0, P9)
    identf = mksb("identf_s", [128, 128], F32, 0, P9)
    iotatl = mksb("iotatl_s", [128, 1], F32, 0, P9)
    iota64 = mksb("iota64_s", [128, 64], F32, 0, P9)
    ones_b = mksb("ones_b", [1, 512], BF16, 0, P9)
    ones_f = mksb("ones_f", [1, 512], F32, 0, P9)
    wr_s = mksb("wr_s", [128, NCH * 2], BF16, 0, P9)
    wg_s = mksb("wg_s", [128, NCH * E8], BF16, 0, P9)
    keys_s = mksb("keys_s", [128, 2 * PH * NKEY], BF16, 0, P9)
    sbb = mksb("sbb_s", [128, 3], F32, 0, P9)
    brr = mksb("brr_s", [1, 2], BF16, 0, P9)
    lsb = mksb("lsb_s", [128, D], F32, 0, P9)
    epsc = mksb("epsc", [128, 1], F32, 0, P9)
    mhalfc = mksb("mhalfc", [128, 1], F32, 0, P9)
    m1c = mksb("m1c", [128, 1], F32, 0, P9)
    onec = mksb("onec", [128, 1], F32, 0, P9)
    st1 = mksb("st1", [128, 1], F32, 0, P9)
    st2 = mksb("st2", [128, 1], F32, 0, P9)
    st3 = mksb("st3", [128, 1], F32, 0, P9)
    st5 = mksb("st5", [128, 1], F32, 0, P9)
    st8 = mksb("st8", [128, 8], F32, 0, P9)
    m8 = mksb("m8", [128, 8], F32, 0, P9)
    t8 = mksb("t8", [128, 8], F32, 0, P9)
    st4w = mksb("st4w", [128, 4], F32, 0, P9)
    al_c = mksb("al_c", [128, NT], F32, 0, P9)
    ws_c = mksb("ws_c", [128, NT], F32, 0, P9)
    cw_c = mksb("cw_c", [128, NT * E8], F32, 0, P9)
    wsl = [mksb(f"wsl{i}", [128, NCH * 128], BF16, 0, P9) for i in range(2)]
    wsl += [mksb(f"wsl{i}", [128, NCH * 128], BF16, PH_C2, PH_C2) for i in range(2, 4)]
    wrh = [mksb(f"wrh{i}", [128, 512], BF16, 0, P9) for i in range(4)]
    zrow = mksb("zrow", [1, 512], F32, 0, P9)
    zbb = mksb("zbb", [64, 512], F32, 0, P9)
    nmrow = mksb("nmrow", [1, TSEQ], BF16, PH_A, PH_B)

    normed = mksb("normed", [128, NCH * TSEQ], BF16, PH_A, PH_C1B)
    xa = mksb("xa", [128, D], F32, PH_A, PH_A)
    xb = mksb("xb", [128, D], F32, PH_A, PH_A)
    xsq = mksb("xsq", [128, D], F32, PH_A, PH_A)
    zt = mksb("zt", [128, D], BF16, PH_A, PH_A)
    qf = mksb("qf", [128, NCH * TSEQ], BF16, PH_B, PH_B)
    kf = mksb("kf", [128, NCH * TSEQ], BF16, PH_B, PH_B)
    vtm = mksb("vtm", [128, NT * NH * 65], BF16, PH_B, PH_B)
    attnO = mksb("attnO", [128, NCH * TSEQ], BF16, PH_B, PH_C1B)
    wv0 = mksb_alias("wv0", [128, NCH * 512], BF16, "attnO", 0)
    wv1 = mksb_alias("wv1", [128, NCH * 512], BF16, "attnO", NCH * 512 * 2)
    expT = mksb("expT", [128, 16 * 256], BF16, PH_B, PH_B)
    xs_c = mksb("xs_c", [128, TSEQ + 3], F32, PH_C1A, PH_C1A)
    ys_c = mksb("ys_c", [128, TSEQ], F32, PH_C1A, PH_C1A)
    sz_c = mksb("sz_c", [128, TSEQ], F32, PH_C1A, PH_C1A)
    u_fm = mksb("u_fm", [128, NCH * TSEQ], BF16, PH_C1A, PH_C1B)
    moe_acc = mksb("moe_acc", [128, NT * D], F32, PH_C2, PH_C3)
    ffm = mksb("ffm", [128, NCH * TSEQ], BF16, PH_C3, PH_D3)
    ue_s = mksb("ue_s", [128, 32 * 1024], BF16, PH_D1, PH_D3)
    xc1 = mksb("xc1", [128, D], F32, PH_C1B, PH_C1B)
    xc2 = mksb("xc2", [128, D], F32, PH_C1B, PH_C1B)
    xc3 = mksb("xc3", [128, D], F32, PH_C1B, PH_C1B)
    xc4 = mksb("xc4", [128, D], F32, PH_C1B, PH_C1B)
    ztc = mksb("ztc", [128, D], BF16, PH_C1B, PH_C1B)
    normed2 = mksb("normed2", [128, NCH * TSEQ], BF16, PH_C1B, PH_C2)
    hfm = mksb("hfm", [128, 16 * 1024], BF16, PH_C2, PH_C2)
    w2r = mksb("w2r", [128, 8 * 1024], BF16, PH_C2, PH_C2)
    msl = mksb("msl", [128, 1024], F32, PH_C2, PH_C2)
    xd1 = mksb("xd1", [128, D], F32, PH_C3, PH_C3)
    xd2 = mksb("xd2", [128, D], F32, PH_C3, PH_C3)
    xd3 = mksb("xd3", [128, D], F32, PH_C3, PH_C3)
    xd4 = mksb("xd4", [128, D], F32, PH_C3, PH_C3)
    ztd = mksb("ztd", [128, D], BF16, PH_C3, PH_C3)
    xsq2 = mksb("xsq2", [128, D], F32, PH_C3, PH_C3)
    qpk = mksb("qpk", [128, 4 * TSEQ], BF16, PH_D1, PH_D2)
    gsp = [mksb(f"gsp{i}", [128, 512], F32, PH_D1, PH_D1) for i in range(2)]
    dsl = [mksb(f"dsl{i}", [128, NCH * 512], BF16, PH_D1, PH_D1) for i in range(2)]
    sims = mksb("sims", [128, 512], F32, PH_D2, PH_D2)
    sel = mksb("sel", [128, 1600], F32, PH_D2, PH_D2)
    selu = mksb("selu", [128, 16], U16, PH_D2, PH_D2)
    seli = mksb("seli", [128, PH * 8], I32, PH_D2, PH_D2)
    selg = mksb("selg", [128, PH * 8], F32, PH_D2, PH_D2)
    selw = mksb("selw", [128, PH * 8], F32, PH_D2, PH_D2)
    wdnr = mksb("wdnr", [128, 32 * 512], BF16, PH_D3, PH_D3)
    rtt = mksb("rtt", [128, 32 * 128], BF16, PH_D3, PH_D3)
    rlb = mksb("rlb", [128, NEXP], BF16, PH_D3, PH_D3)
    hfd = mksb("hfd", [128, 32 * 512], BF16, PH_D3, PH_D3)
    xe1 = mksb("xe1", [128, 512], F32, PH_D3, PH_D3)
    xe2 = mksb("xe2", [128, 512], F32, PH_D3, PH_D3)
    zerf = mksb("zerf", [128, NEXP], F32, PH_P0, PH_P0)

    with ExitStack() as ctx:
        EE = ctx.enter_context
        ps_big = EE(nc.psum_tensor("ps_big", [128, 2048], F32))
        ps_a = EE(nc.psum_tensor("ps_a", [128, 512], F32))
        ps_b = EE(nc.psum_tensor("ps_b", [128, 512], F32))
        ps_t0 = EE(nc.psum_tensor("ps_t0", [128, 256], BF16))
        ps_t1 = EE(nc.psum_tensor("ps_t1", [128, 256], BF16))
        psq = [ps_big.ap()[:, i * 512:(i + 1) * 512] for i in range(4)]

        sems, cnt = {}, {}
        SEMN = ["ev", "vt", "sv", "go", "gsc", "rdz", "mis", "msg", "lsr", "tap",
                "xa", "nsp", "xc4", "h2s", "xdl", "x1s", "ue", "wdnr", "rlb",
                "xe1", "out", "w2r", "dsl0", "dsl1", "gsp0", "gsp1",
                "wsl0", "wsl1", "wsl2", "wsl3", "wv0", "wv1", "wrh0", "wrh1", "wrh2", "wrh3"]
        for name in SEMN:
            sems[name] = EE(nc.semaphore(name))
            cnt[name] = 0
        block = EE(nc.Block())

        def inc(inst, name, k=1):
            inst.then_inc(sems[name], k)
            cnt[name] += k

        ENG = ["sync", "gpsimd", "vector", "scalar", "tensor"]

        def barrier():
            snap = {n: cnt[n] for n in SEMN}
            for en in ENG:
                def emit(e, en=en):
                    for n in SEMN:
                        if snap[n]:
                            e.wait_ge(sems[n], snap[n])
                getattr(block, en)(emit)

        class Chained:
            PASS = {"wait_ge", "dma_start", "sem_inc", "nop", "engine_nop",
                    "indirect_dma_start"}

            def __init__(self, e, sem):
                self._e, self._sem = e, sem

            def __getattr__(self, name):
                fn = getattr(self._e, name)
                if name in Chained.PASS or name.startswith("_") or not callable(fn):
                    return fn

                def wrapped(*a, **k):
                    self._e.wait_ge(sems[self._sem], cnt[self._sem])
                    r = fn(*a, **k)
                    inc(r, self._sem)
                    return r
                return wrapped

        S, G_, T = block.sync, block.gpsimd, block.tensor

        def V(f):
            block.vector(lambda v: f(Chained(v, "vt")))

        def SC(f):
            block.scalar(lambda sc: f(Chained(sc, "sv")))

        def fmv(buf, c, t0, n):
            return buf.ap()[:, c * TSEQ + t0: c * TSEQ + t0 + n]

        SLOTSEM = {id(wsl[0]): "wsl0", id(wsl[1]): "wsl1",
                   id(wsl[2]): "wsl2", id(wsl[3]): "wsl3",
                   id(wv0): "wv0", id(wv1): "wv1",
                   id(wrh[0]): "wrh0", id(wrh[1]): "wrh1",
                   id(wrh[2]): "wrh2", id(wrh[3]): "wrh3",
                   id(dsl[0]): "dsl0", id(dsl[1]): "dsl1",
                   id(gsp[0]): "gsp0", id(gsp[1]): "gsp1"}

        def ssem(slot):
            return SLOTSEM[id(slot)]

        def wait_dma(e, *streams):
            for st_ in streams:
                if cnt[st_]:
                    e.wait_ge(sems[st_], cnt[st_])

        def load_lhsT(g, slot, wdram, col0, ncol=128):
            inc(g.dma_start(
                out=slot.ap()[:, 0:NCH * ncol].rearrange("p (c n) -> p c n", n=ncol),
                in_=wdram[:, col0:col0 + ncol].rearrange("(c p) n -> p c n", p=128)),
                ssem(slot), 16)

        def load_rhs_blk(g, slot, wdram, n0):
            inc(g.dma_start(
                out=slot.ap().rearrange("p (c n) -> p c n", n=512),
                in_=wdram[:, n0:n0 + 512].rearrange("(c p) n -> p c n", p=128)),
                ssem(slot), 16)

        def load_rhs(g, slot, wdram, c, n0, nw):
            inc(g.dma_start(out=slot.ap()[:, 0:nw],
                            in_=wdram[c * 128:(c + 1) * 128, n0:n0 + nw]), ssem(slot), 16)

        def ln_stats(tile_ap, sq_ap, pre=None):
            def _(v):
                if pre is not None:
                    wait_dma(v, pre)
                v.reduce_sum(out=st1.ap(), in_=tile_ap, axis=AX.X)
            V(_)

            def _(sc):
                sc.wait_ge(sems["vt"], cnt["vt"])
                sc.activation(out=sq_ap, in_=tile_ap, func=AF.Square,
                                  accum_out=st2.ap())
            SC(_)

            def _(v):
                v.wait_ge(sems["sv"], cnt["sv"])
                v.tensor_scalar_mul(out=st1.ap(), in0=st1.ap(), scalar1=INV_D)
                v.tensor_scalar_mul(out=st2.ap(), in0=st2.ap(), scalar1=INV_D)
                v.tensor_tensor(out=st3.ap(), in0=st1.ap(), in1=st1.ap(), op=ALU.mult)
                v.tensor_sub(out=st2.ap(), in0=st2.ap(), in1=st3.ap())
            V(_)

            def _(sc):
                sc.wait_ge(sems["vt"], cnt["vt"])
                sc.activation(out=st3.ap(), in_=st2.ap(), func=AF.Ln, bias=epsc.ap())
                sc.activation(out=st3.ap(), in_=st3.ap(), func=AF.Exp,
                                  scale=mhalfc.ap())
            SC(_)

        def transpose_to_fm(src_ap, dst_buf, t0, gc=None, bc=None):
            for c in range(NCH):
                pst = [ps_t0, ps_t1][c % 2]

                def _(t, c=c, pst=pst):
                    if c < 2:
                        t.wait_ge(sems["vt"], cnt["vt"])
                    else:
                        t.wait_ge(sems["vt"], cnt["vt"] - 1)
                    inc(t.transpose(pst.ap()[:, 0:128], src_ap[:, c * 128:(c + 1) * 128],
                                    identb.ap()), "ev")
                T(_)

                def _(v, c=c, pst=pst):
                    v.wait_ge(sems["ev"], cnt["ev"])
                    dst = fmv(dst_buf, c, t0, 128)
                    if gc is not None:
                        v.tensor_scalar(out=dst, in0=pst.ap()[:, 0:128],
                                            scalar1=gc.ap()[:, c:c + 1],
                                            scalar2=bc.ap()[:, c:c + 1],
                                            op0=ALU.mult, op1=ALU.add)
                    else:
                        v.tensor_copy(out=dst, in_=pst.ap()[:, 0:128])
                V(_)

        def tap_bf16(name, buf, ncols):
            if name not in tap_out:
                return
            barrier()
            for cix in range(ncols // 1024):
                def _(v, cix=cix):
                    wait_dma(v, "tap")
                    v.tensor_copy(out=lsb.ap(),
                                      in_=buf.ap()[:, cix * 1024:(cix + 1) * 1024])
                V(_)

                def _(sync, cix=cix):
                    sync.wait_ge(sems["vt"], cnt["vt"])
                    inc(sync.dma_start(out=tap_out[name][:, cix * 1024:(cix + 1) * 1024],
                                       in_=lsb.ap()), "tap", 16)
                S(_)
            barrier()

        def tap_f32(name, buf, ncols):
            if name not in tap_out:
                return
            barrier()

            def _(sync):
                inc(sync.dma_start(out=tap_out[name][:, 0:ncols],
                                   in_=buf.ap()[:, 0:ncols]), "tap", 16)
            S(_)
            barrier()

        def _(sync):
            for dst, src in [(g1c, G1C), (b1c, B1C), (g2c, G2C), (b2c, B2C),
                             (cbc, CBC), (cwt, CWT), (identb, IDENTB), (identf, IDENTF),
                             (iotatl, IOTATL), (iota64, IOTA64)]:
                inc(sync.dma_start(out=dst.ap(), in_=src), "mis", 16)
            inc(sync.dma_start(out=sbb.ap()[0:1, 0:3], in_=SBROW), "mis", 16)
        S(_)

        def _(g):
            inc(g.dma_start(out=wr_s.ap().rearrange("p (c k) -> p c k", k=2),
                            in_=WR.rearrange("(c p) k -> p c k", p=128)), "msg", 16)
            inc(g.dma_start(out=wg_s.ap().rearrange("p (c k) -> p c k", k=E8),
                            in_=WG8.rearrange("(c p) k -> p c k", p=128)), "msg", 16)
            inc(g.dma_start(out=keys_s.ap(), in_=KEYSQ), "msg", 16)
            inc(g.dma_start(out=brr.ap(), in_=BRROW), "msg", 16)
        G_(_)
        barrier()

        def _(v):
            v.memset(ones_b.ap(), 1.0)
            v.memset(ones_f.ap(), 1.0)
            v.memset(zerf.ap(), 0.0)
            v.memset(epsc.ap(), EPS)
            v.memset(mhalfc.ap(), -0.5)
            v.memset(m1c.ap(), -1.0)
            v.memset(onec.ap(), 1.0)
        V(_)
        barrier()

        def _(t):
            inc(t.matmul(ps_a.ap()[:, 0:3], ones_f.ap()[0:1, 0:128], sbb.ap()[0:1, 0:3],
                         start=True, stop=True), "ev")
        T(_)

        def _(v):
            v.wait_ge(sems["ev"], cnt["ev"])
            v.tensor_copy(out=sbb.ap()[:, 0:3], in_=ps_a.ap()[:, 0:3])
        V(_)

        def _(sc):
            for i in range(TCORE // 128):
                inc(sc.dma_start(out=RDD[i * 128:(i + 1) * 128, :], in_=zerf.ap()),
                    "rdz", 16)
        SC(_)
        barrier()

        def build_ls(src_row):
            def _(sync):
                sync.wait_ge(sems["vt"], cnt["vt"])
                inc(sync.dma_start(out=zrow.ap(), in_=src_row[0:1, 0:512]), "lsr", 16)
                inc(sync.dma_start(out=zbb.ap()[0:1, :], in_=src_row[0:1, 512:1024]),
                    "lsr", 16)
            S(_)

            def _(t):
                wait_dma(t, "lsr")
                t.matmul(ps_a.ap(), ones_f.ap()[0:1, 0:128], zrow.ap(),
                         start=True, stop=True)
                inc(t.matmul(ps_b.ap(), ones_f.ap()[0:1, 0:128], zbb.ap()[0:1, :],
                             start=True, stop=True), "ev")
            T(_)

            def _(v):
                v.wait_ge(sems["ev"], cnt["ev"])
                v.tensor_copy(out=lsb.ap()[:, 0:512], in_=ps_a.ap())
                v.tensor_copy(out=lsb.ap()[:, 512:1024], in_=ps_b.ap())
            V(_)

        def emit_c_phases(s, SOFF):
            for fo in range(NCH):
                def _(g):
                    g.wait_ge(sems["ev"], cnt["ev"])
                    load_lhsT(g, wsl[0], WIN, fo * 128)
                    load_lhsT(g, wsl[1], WIN, 1024 + fo * 128)
                G_(_)

                def _(t):
                    wait_dma(t, "wsl0", "wsl1")
                    t.wait_ge(sems["vt"], cnt["vt"])
                    last = None
                    for tb in range(4):
                        for c in range(NCH):
                            last = t.matmul(psq[tb], wsl[0].ap()[:, c * 128:(c + 1) * 128],
                                            fmv(normed, c, tb * 512, 512),
                                            start=(c == 0), stop=(c == NCH - 1))
                    inc(last, "ev")
                T(_)

                def _(v):
                    v.wait_ge(sems["ev"], cnt["ev"])
                    for tb in range(4):
                        v.tensor_copy(out=xs_c.ap()[:, 3 + tb * 512:3 + (tb + 1) * 512],
                                      in_=psq[tb])
                    v.memset(xs_c.ap()[:, 0:3], 0.0)
                V(_)

                def _(t):
                    t.wait_ge(sems["vt"], cnt["vt"])
                    last = None
                    for tb in range(4):
                        for c in range(NCH):
                            last = t.matmul(psq[tb], wsl[1].ap()[:, c * 128:(c + 1) * 128],
                                            fmv(normed, c, tb * 512, 512),
                                            start=(c == 0), stop=(c == NCH - 1))
                    inc(last, "ev")
                T(_)

                def _(sc):
                    sc.wait_ge(sems["ev"], cnt["ev"])
                    last = None
                    for tb in range(4):
                        last = sc.activation(out=sz_c.ap()[:, tb * 512:(tb + 1) * 512],
                                             in_=psq[tb], func=AF.Sigmoid)
                    pass
                SC(_)

                def _(v, fo=fo):
                    v.wait_ge(sems["sv"], cnt["sv"])
                    v.tensor_scalar_mul(out=ys_c.ap(), in0=xs_c.ap()[:, 0:TSEQ],
                                        scalar1=cwt.ap()[:, fo * 4:fo * 4 + 1])
                    last = None
                    for k in (1, 2, 3):
                        last = v.scalar_tensor_tensor(
                            out=ys_c.ap(), in0=xs_c.ap()[:, k:k + TSEQ],
                            scalar=cwt.ap()[:, fo * 4 + k:fo * 4 + k + 1],
                            in1=ys_c.ap(), op0=ALU.mult, op1=ALU.add)
                    pass
                V(_)

                def _(sc, fo=fo):
                    sc.wait_ge(sems["vt"], cnt["vt"])
                    sc.activation(out=ys_c.ap(), in_=ys_c.ap(), func=AF.Silu,
                                      bias=cbc.ap()[:, fo:fo + 1])
                SC(_)

                def _(v, fo=fo):
                    v.wait_ge(sems["sv"], cnt["sv"])
                    v.tensor_tensor(out=fmv(u_fm, fo, 0, TSEQ), in0=ys_c.ap(),
                                        in1=sz_c.ap(), op=ALU.mult)
                V(_)
            barrier()
            if s == 0:
                tap_bf16("u", u_fm, NCH * TSEQ)

            build_ls(LS1R)
            for i in range(NT):
                R0 = SOFF + i * 128

                def _(sync):
                    sync.wait_ge(sems["vt"], cnt["vt"])
                    sync.wait_ge(sems["sv"], cnt["sv"])
                    inc(sync.dma_start(out=xc4.ap(), in_=NIN_D[R0:R0 + 128, :]), "xc4", 16)
                S(_)

                for half in range(2):
                    for which, (lhs_buf, wdram) in enumerate(
                            [(normed, WGATE), (attnO, WOUT), (u_fm, WOS)]):
                        for c in range(NCH):
                            slot = wrh[c % 2]

                            def _(g, c=c, slot=slot, wdram=wdram, half=half):
                                g.wait_ge(sems["ev"], max(0, cnt["ev"] - 1))
                                load_rhs(g, slot, wdram, c, half * 512, 512)
                            G_(_)

                            def _(t, c=c, which=which, lhs_buf=lhs_buf, slot=slot, i=i):
                                wait_dma(t, ssem(slot))
                                if c == 0 and which == 0:
                                    t.wait_ge(sems["vt"], cnt["vt"])
                                inc(t.matmul(psq[which], fmv(lhs_buf, c, i * 128, 128),
                                             slot.ap(), start=(c == 0),
                                             stop=(c == NCH - 1)), "ev")
                            T(_)

                    def _(sc, half=half):
                        sc.wait_ge(sems["ev"], cnt["ev"])
                        sc.activation(out=xc3.ap()[:, half * 512:(half + 1) * 512],
                                          in_=psq[0], func=AF.Sigmoid)
                    SC(_)

                    def _(v, half=half, i=i):
                        v.wait_ge(sems["sv"], cnt["sv"])
                        wait_dma(v, "xc4", "h2s")
                        hs = slice(half * 512, (half + 1) * 512)
                        v.tensor_tensor(out=xc1.ap()[:, hs], in0=psq[1],
                                        in1=xc3.ap()[:, hs], op=ALU.mult)
                        v.tensor_tensor(out=xc3.ap()[:, hs], in0=xc1.ap()[:, hs],
                                        in1=psq[2], op=ALU.subtract)
                        v.scalar_tensor_tensor(out=xc3.ap()[:, hs], in0=xc3.ap()[:, hs],
                                               scalar=al_c.ap()[:, i:i + 1], in1=psq[2],
                                               op0=ALU.mult, op1=ALU.add)
                        v.tensor_tensor(out=xc3.ap()[:, hs], in0=xc3.ap()[:, hs],
                                        in1=lsb.ap()[:, hs], op=ALU.mult)
                        v.tensor_tensor(out=xc2.ap()[:, hs], in0=xc4.ap()[:, hs],
                                            in1=xc3.ap()[:, hs], op=ALU.add)
                    V(_)

                def _(sync, i=i):
                    sync.wait_ge(sems["vt"], cnt["vt"])
                    inc(sync.dma_start(out=H2_D[i * 128:(i + 1) * 128, :], in_=xc2.ap()),
                        "h2s", 16)
                S(_)

                ln_stats(xc2.ap(), xc1.ap())

                def _(v):
                    v.wait_ge(sems["sv"], cnt["sv"])
                    v.tensor_scalar(out=ztc.ap(), in0=xc2.ap(), scalar1=st1.ap(),
                                        scalar2=st3.ap(), op0=ALU.subtract,
                                        op1=ALU.mult)
                V(_)
                transpose_to_fm(ztc.ap(), normed2, i * 128, g2c, b2c)

                def _(t, i=i):
                    t.wait_ge(sems["vt"], cnt["vt"])
                    for c in range(NCH):
                        last = t.matmul(ps_a.ap()[:, 0:8], fmv(normed2, c, i * 128, 128),
                                        wg_s.ap()[:, c * 8:(c + 1) * 8],
                                        start=(c == 0), stop=(c == NCH - 1))
                    inc(last, "ev")
                T(_)

                def _(v):
                    v.wait_ge(sems["ev"], cnt["ev"])
                    v.tensor_copy(out=st8.ap(), in_=ps_a.ap()[:, 0:8])
                    v.max(out=m8.ap(), in_=st8.ap())
                    v.tensor_scalar_mul(out=st1.ap(), in0=m8.ap()[:, 0:1],
                                            scalar1=-1.0)
                V(_)

                def _(sc):
                    sc.wait_ge(sems["vt"], cnt["vt"])
                    sc.activation(out=t8.ap(), in_=st8.ap(), func=AF.Exp,
                                  bias=st1.ap(), accum_out=st2.ap())
                    sc.activation(out=st3.ap(), in_=m8.ap()[:, 1:2], func=AF.Exp,
                                      bias=st1.ap())
                SC(_)

                def _(v, i=i):
                    v.wait_ge(sems["sv"], cnt["sv"])
                    v.reciprocal(out=st2.ap(), in_=st2.ap())
                    v.tensor_scalar(out=t8.ap(), in0=t8.ap(), scalar1=st2.ap(),
                                    scalar2=None, op0=ALU.mult)
                    v.tensor_tensor(out=st3.ap(), in0=st3.ap(), in1=st2.ap(),
                                    op=ALU.mult)
                    v.tensor_scalar(out=m8.ap(), in0=t8.ap(), scalar1=st3.ap(),
                                    scalar2=None, op0=ALU.is_ge)
                    v.tensor_tensor(out=st5.ap(), in0=st2.ap(), in1=st3.ap(), op=ALU.add)
                    v.tensor_scalar(out=st5.ap(), in0=st5.ap(), scalar1=1e-8,
                                    scalar2=None, op0=ALU.add)
                    v.reciprocal(out=st5.ap(), in_=st5.ap())
                    v.tensor_tensor(out=t8.ap(), in0=t8.ap(), in1=m8.ap(), op=ALU.mult)
                    v.tensor_scalar(out=cw_c.ap()[:, i * 8:(i + 1) * 8], in0=t8.ap(),
                                        scalar1=st5.ap(), scalar2=None, op0=ALU.mult)
                V(_)
            barrier()
            if s == 0:
                tap_bf16("normed2", normed2, NCH * TSEQ)
                tap_f32("cw", cw_c, NT * E8)
            if STAGE[0] < 4:
                return

            for hh in range(2):
                T0 = hh * 1024
                for e in range(E8):
                    tmark = [cnt["ev"], cnt["ev"]]
                    for fo in range(16):
                        pair = fo % 2
                        w1s, w3s = wsl[pair * 2], wsl[pair * 2 + 1]

                        def _(g, e=e, fo=fo, pair=pair, w1s=w1s, w3s=w3s):
                            g.wait_ge(sems["ev"], tmark[pair])
                            load_lhsT(g, w1s, W1[e], fo * 128)
                            load_lhsT(g, w3s, W3[e], fo * 128)
                        G_(_)

                        def _(t, fo=fo, T0=T0, pair=pair, w1s=w1s, w3s=w3s):
                            wait_dma(t, ssem(w1s), ssem(w3s))
                            t.wait_ge(sems["vt"], cnt["vt"])
                            t.wait_ge(sems["sv"], cnt["sv"])
                            last = None
                            for tb in range(2):
                                for c in range(NCH):
                                    t.matmul(psq[tb],
                                             w1s.ap()[:, c * 128:(c + 1) * 128],
                                             fmv(normed2, c, T0 + tb * 512, 512),
                                             start=(c == 0), stop=(c == NCH - 1))
                                for c in range(NCH):
                                    last = t.matmul(
                                        psq[2 + tb], w3s.ap()[:, c * 128:(c + 1) * 128],
                                        fmv(normed2, c, T0 + tb * 512, 512),
                                        start=(c == 0), stop=(c == NCH - 1))
                            inc(last, "ev")
                            tmark[pair] = cnt["ev"]
                        T(_)

                        for tb in range(2):
                            def _(sc, tb=tb):
                                sc.wait_ge(sems["ev"], cnt["ev"])
                                sc.wait_ge(sems["vt"], cnt["vt"] - 1 if tb else cnt["vt"])
                                sc.activation(
                                    out=msl.ap()[:, tb * 512:(tb + 1) * 512],
                                    in_=psq[tb], func=AF.Silu)
                            SC(_)

                            def _(v, tb=tb, fo=fo):
                                v.wait_ge(sems["sv"], cnt["sv"])
                                v.tensor_tensor(
                                    out=hfm.ap()[:, fo * 1024 + tb * 512:
                                                 fo * 1024 + (tb + 1) * 512],
                                    in0=msl.ap()[:, tb * 512:(tb + 1) * 512],
                                    in1=psq[2 + tb], op=ALU.mult)
                            V(_)

                    for w2h in range(2):
                        def _(g, e=e, w2h=w2h):
                            g.wait_ge(sems["ev"], cnt["ev"])
                            inc(g.dma_start(
                                out=w2r.ap().rearrange("p (c n) -> p c n", n=1024),
                                in_=W2[e][w2h * 1024:(w2h + 1) * 1024, :]
                                .rearrange("(c p) n -> p c n", p=128)), "w2r", 16)
                        G_(_)

                        for tb in range(8):
                            ti = hh * 8 + tb
                            for dh in range(2):
                                pq = psq[dh]

                                def _(t, tb=tb, dh=dh, w2h=w2h, pq=pq):
                                    wait_dma(t, "w2r")
                                    if tb == 0 and dh == 0:
                                        t.wait_ge(sems["vt"], cnt["vt"])
                                    else:
                                        t.wait_ge(sems["vt"], cnt["vt"] - 1)
                                    last = None
                                    for j in range(8):
                                        hc = w2h * 8 + j
                                        last = t.matmul(
                                            pq,
                                            hfm.ap()[:, hc * 1024 + tb * 128:
                                                     hc * 1024 + tb * 128 + 128],
                                            w2r.ap()[:, j * 1024 + dh * 512:
                                                     j * 1024 + dh * 512 + 512],
                                            start=(j == 0), stop=(j == 7))
                                    inc(last, "ev")
                                T(_)

                                def _(v, ti=ti, dh=dh, e=e, w2h=w2h, pq=pq):
                                    v.wait_ge(sems["ev"], cnt["ev"])
                                    dst = moe_acc.ap()[:, ti * D + dh * 512:
                                                       ti * D + (dh + 1) * 512]
                                    cwcol = cw_c.ap()[:, ti * 8 + e: ti * 8 + e + 1]
                                    if e == 0 and w2h == 0:
                                        v.tensor_scalar(out=dst, in0=pq,
                                                            scalar1=cwcol, scalar2=None,
                                                            op0=ALU.mult)
                                    else:
                                        v.scalar_tensor_tensor(
                                            out=dst, in0=pq, scalar=cwcol, in1=dst,
                                            op0=ALU.mult, op1=ALU.add)
                                V(_)
            barrier()
            if s == 0:
                tap_f32("moe", moe_acc, NT * D)

            build_ls(LS2R)
            for i in range(NT):
                R0 = SOFF + i * 128

                def _(sync, i=i, R0=R0):
                    sync.wait_ge(sems["vt"], cnt["vt"])
                    sync.wait_ge(sems["sv"], cnt["sv"])
                    wait_dma(sync, "x1s")
                    inc(sync.dma_start(out=xd1.ap(), in_=X0[R0:R0 + 128, :]), "xdl", 16)
                    inc(sync.dma_start(out=xd2.ap(), in_=NIN_D[R0:R0 + 128, :]), "xdl", 16)
                    inc(sync.dma_start(out=xd3.ap(), in_=H2_D[i * 128:(i + 1) * 128, :]),
                        "xdl", 16)
                S(_)

                def _(v, i=i):
                    wait_dma(v, "xdl")
                    v.tensor_tensor(out=xd4.ap(), in0=moe_acc.ap()[:, i * D:(i + 1) * D],
                                    in1=lsb.ap(), op=ALU.mult)
                    v.tensor_tensor(out=xd4.ap(), in0=xd3.ap(), in1=xd4.ap(), op=ALU.add)
                    v.scalar_tensor_tensor(out=xd4.ap(), in0=xd4.ap(),
                                           scalar=ws_c.ap()[:, i:i + 1], in1=xd2.ap(),
                                           op0=ALU.mult, op1=ALU.add)
                    v.tensor_tensor(out=xd1.ap(), in0=xd1.ap(), in1=xd4.ap(),
                                        op=ALU.add)
                V(_)

                def _(sync, i=i):
                    sync.wait_ge(sems["vt"], cnt["vt"])
                    inc(sync.dma_start(out=X1_D[i * 128:(i + 1) * 128, :], in_=xd1.ap()),
                        "x1s", 16)
                S(_)

                ln_stats(xd1.ap(), xsq2.ap())

                def _(v):
                    v.wait_ge(sems["sv"], cnt["sv"])
                    v.tensor_scalar(out=ztd.ap(), in0=xd1.ap(), scalar1=st1.ap(),
                                        scalar2=st3.ap(), op0=ALU.subtract,
                                        op1=ALU.mult)
                V(_)
                transpose_to_fm(ztd.ap(), ffm, i * 128)
            barrier()
            if s == 0:
                tap_bf16("ffm", ffm, NCH * TSEQ)

        def emit_d_phases(s, SOFF):
            def _(g):
                inc(g.dma_start(out=ue_s.ap().rearrange("p (c n) -> p c n", n=1024),
                                in_=UED.rearrange("(c p) n -> p c n", p=128)), "ue", 16)
            G_(_)

            for qo in range(4):
                def _(g, qo=qo):
                    g.wait_ge(sems["ev"], cnt["ev"])
                    load_lhsT(g, wsl[qo % 2], WQT, qo * 128)
                G_(_)

                def _(t, qo=qo):
                    wait_dma(t, ["wsl0", "wsl1"][qo % 2])
                    t.wait_ge(sems["vt"], cnt["vt"])
                    last = None
                    for tb in range(4):
                        for c in range(NCH):
                            last = t.matmul(psq[tb],
                                            wsl[qo % 2].ap()[:, c * 128:(c + 1) * 128],
                                            fmv(ffm, c, tb * 512, 512),
                                            start=(c == 0), stop=(c == NCH - 1))
                    inc(last, "ev")
                T(_)

                def _(v, qo=qo):
                    v.wait_ge(sems["ev"], cnt["ev"])
                    last = None
                    for tb in range(4):
                        last = v.tensor_copy(out=qpk.ap()[:, qo * TSEQ + tb * 512:
                                                          qo * TSEQ + (tb + 1) * 512],
                                             in_=psq[tb])
                    pass
                V(_)

            for eo in range(8):
                def _(g, eo=eo):
                    g.wait_ge(sems["ev"], cnt["ev"] - 1 if eo else cnt["ev"])
                    inc(g.dma_start(
                        out=dsl[eo % 2].ap().rearrange("p (c n) -> p c n", n=512),
                        in_=DET[:, eo * 512:(eo + 1) * 512]
                        .rearrange("(c p) n -> p c n", p=128)), ["dsl0", "dsl1"][eo % 2], 16)
                G_(_)

                for tb in range(NT):
                    pq = psq[tb % 2]
                    gb = gsp[tb % 2]

                    def _(t, eo=eo, tb=tb, pq=pq):
                        wait_dma(t, ["dsl0", "dsl1"][eo % 2])
                        if tb < 2:
                            t.wait_ge(sems["vt"], cnt["vt"])
                        else:
                            t.wait_ge(sems["vt"], cnt["vt"] - 1)
                        last = None
                        for c in range(NCH):
                            last = t.matmul(pq, fmv(ffm, c, tb * 128, 128),
                                            dsl[eo % 2].ap()[:, c * 512:(c + 1) * 512],
                                            start=(c == 0), stop=(c == NCH - 1))
                        inc(last, "ev")
                    T(_)

                    def _(v, tb=tb, pq=pq, gb=gb):
                        v.wait_ge(sems["ev"], cnt["ev"])
                        wait_dma(v, ssem(gb))
                        v.tensor_copy(out=gb.ap(), in_=pq)
                    V(_)

                    def _(sync, eo=eo, tb=tb, gb=gb):
                        sync.wait_ge(sems["vt"], cnt["vt"])
                        inc(sync.dma_start(
                            out=GDD[tb * 128:(tb + 1) * 128, eo * 512:(eo + 1) * 512],
                            in_=gb.ap()), ssem(gb), 16)
                    S(_)
            barrier()
            if s == 0:
                tap_bf16("qpk", qpk, 4 * TSEQ)
            if STAGE[0] < 6:
                return

            SL = {"mx8": 0, "my8": 8, "ixf": 16, "iyp": 24, "cand": 32, "c8": 96,
                  "mask": 104, "eofc": 168, "em": 232, "e8p": 296, "pexp": 304,
                  "pm": 368, "EQ": 432, "PP": 944, "p8": 1456, "a8": 1464, "idxf": 1472}

            def slv(nm, w):
                o = SL[nm]
                return sel.ap()[:, o:o + w]

            for tb in range(NT):
                def _(t, tb=tb):
                    t.wait_ge(sems["vt"], cnt["vt"])
                    for p in range(2):
                        for hh in range(PH):
                            qo = p * 2 + hh // 2
                            rb2 = 64 * (hh % 2)
                            if p or hh:
                                t.wait_ge(sems["ev"], cnt["ev"])
                            inc(t.matmul(
                                ps_a.ap()[:, (p * 4 + hh) * 64:(p * 4 + hh) * 64 + 64],
                                qpk.ap()[rb2:rb2 + 64, qo * TSEQ + tb * 128:
                                         qo * TSEQ + tb * 128 + 128],
                                keys_s.ap()[rb2:rb2 + 64,
                                            (p * 4 + hh) * 64:(p * 4 + hh) * 64 + 64],
                                start=True, stop=True), "ev")
                T(_)

                def _(v, tb=tb):
                    v.wait_ge(sems["ev"], cnt["ev"])
                    v.tensor_copy(out=sims.ap(), in_=ps_a.ap())
                V(_)

                for hh in range(PH if D2LIM[0] >= 2 else 0):
                    sx = sims.ap()[:, hh * 64:(hh + 1) * 64]
                    sy = sims.ap()[:, (PH + hh) * 64:(PH + hh + 1) * 64]

                    def _(v, tb=tb, hh=hh, sx=sx, sy=sy):
                        v.wait_ge(sems["go"], cnt["go"])
                        v.max(out=slv("mx8", 8), in_=sx)
                        v.max_index(out=selu.ap()[:, 0:8], in_max=slv("mx8", 8),
                                    in_values=sx)
                        v.max(out=slv("my8", 8), in_=sy)
                        v.max_index(out=selu.ap()[:, 8:16], in_max=slv("my8", 8),
                                    in_values=sy)
                        v.tensor_copy(out=slv("ixf", 8), in_=selu.ap()[:, 0:8])
                        v.tensor_copy(out=slv("iyp", 8), in_=selu.ap()[:, 8:16])
                        v.tensor_scalar_add(out=slv("iyp", 8), in0=slv("iyp", 8),
                                            scalar1=1.0)
                        rep = slv("mx8", 8).rearrange("p (r o) -> p r o", o=1) \
                            .to_broadcast([128, 8, 8])
                        til = slv("my8", 8).rearrange("p (o r) -> p o r", o=1) \
                            .to_broadcast([128, 8, 8])
                        v.tensor_tensor(
                            out=slv("cand", 64).rearrange("p (r c) -> p r c", c=8),
                            in0=rep, in1=til, op=ALU.add)
                        v.max(out=slv("c8", 8), in_=slv("cand", 64))
                        v.tensor_scalar(out=slv("mask", 64), in0=slv("cand", 64),
                                        scalar1=slv("c8", 8)[:, 7:8], scalar2=None,
                                        op0=ALU.is_ge)
                        repx = slv("ixf", 8).rearrange("p (r o) -> p r o", o=1) \
                            .to_broadcast([128, 8, 8])
                        tily = slv("iyp", 8).rearrange("p (o r) -> p o r", o=1) \
                            .to_broadcast([128, 8, 8])
                        v.scalar_tensor_tensor(
                            out=slv("eofc", 64).rearrange("p (r c) -> p r c", c=8),
                            in0=repx, scalar=64.0, in1=tily, op0=ALU.mult, op1=ALU.add)
                        v.tensor_tensor(out=slv("em", 64), in0=slv("eofc", 64),
                                        in1=slv("mask", 64), op=ALU.mult)
                        v.max(out=slv("e8p", 8), in_=slv("em", 64))
                        v.tensor_scalar_mul(out=st1.ap(), in0=slv("c8", 8)[:, 0:1],
                                                scalar1=-1.0)
                    V(_)

                    def _(sc):
                        sc.wait_ge(sems["vt"], cnt["vt"])
                        sc.activation(out=slv("pexp", 64), in_=slv("cand", 64),
                                          func=AF.Exp, bias=st1.ap())
                    SC(_)

                    def _(v, tb=tb, hh=hh):
                        v.wait_ge(sems["sv"], cnt["sv"])
                        v.tensor_tensor(out=slv("pm", 64), in0=slv("pexp", 64),
                                        in1=slv("mask", 64), op=ALU.mult)
                        v.reduce_sum(out=st2.ap(), in_=slv("pm", 64), axis=AX.X)
                        v.reciprocal(out=st2.ap(), in_=st2.ap())
                        v.tensor_scalar(out=slv("pm", 64), in0=slv("pm", 64),
                                        scalar1=st2.ap(), scalar2=None, op0=ALU.mult)
                        repe = slv("e8p", 8).rearrange("p (j o) -> p j o", o=1) \
                            .to_broadcast([128, 8, 64])
                        bce = slv("eofc", 64).rearrange("p (o c) -> p o c", o=1) \
                            .to_broadcast([128, 8, 64])
                        v.tensor_tensor(
                            out=slv("EQ", 512).rearrange("p (j c) -> p j c", c=64),
                            in0=repe, in1=bce, op=ALU.is_equal)
                        bcp = slv("pm", 64).rearrange("p (o c) -> p o c", o=1) \
                            .to_broadcast([128, 8, 64])
                        v.tensor_tensor(
                            out=slv("PP", 512).rearrange("p (j c) -> p j c", c=64),
                            in0=slv("EQ", 512).rearrange("p (j c) -> p j c", c=64),
                            in1=bcp, op=ALU.mult)
                        v.reduce_sum(out=slv("p8", 8),
                                     in_=slv("PP", 512).rearrange("p (j c) -> p j c", c=64),
                                     axis=AX.X)
                        v.tensor_scalar_add(out=slv("a8", 8), in0=slv("e8p", 8),
                                            scalar1=-1.0)
                        iob = iotatl.ap().to_broadcast([128, 8])
                        v.scalar_tensor_tensor(out=slv("idxf", 8), in0=iob, scalar=4096.0,
                                               in1=slv("a8", 8), op0=ALU.mult, op1=ALU.add)
                        v.tensor_copy(out=seli.ap()[:, hh * 8:(hh + 1) * 8],
                                          in_=slv("idxf", 8))
                    V(_)

                    def _(g, tb=tb, hh=hh):
                        if D2LIM[0] < 3:
                            return
                        g.wait_ge(sems["vt"], cnt["vt"])
                        g.wait_ge(sems["go"], cnt["go"])
                        for j in range(8):
                            inc(g.indirect_dma_start(
                                out=selg.ap()[:, hh * 8 + j: hh * 8 + j + 1],
                                out_offset=None,
                                in_=GD_flat,
                                in_offset=IndirectOffsetOnAxis(
                                    seli.ap()[:, hh * 8 + j: hh * 8 + j + 1], 0),
                                element_offset=tb * 128 * NEXP), "go", 16)
                    G_(_)

                    def _(v, tb=tb, hh=hh):
                        if D2LIM[0] < 3:
                            return
                        v.wait_ge(sems["go"], cnt["go"])
                        v.tensor_tensor(out=t8.ap(),
                                            in0=selg.ap()[:, hh * 8:(hh + 1) * 8],
                                            in1=slv("p8", 8), op=ALU.mult)
                    V(_)

                    def _(sc, hh=hh):
                        if D2LIM[0] < 3:
                            return
                        sc.wait_ge(sems["vt"], cnt["vt"])
                        sc.activation(out=selw.ap()[:, hh * 8:(hh + 1) * 8],
                                          in_=t8.ap(), func=AF.Silu)
                    SC(_)

                    def _(g, tb=tb, hh=hh, SOFF=SOFF):
                        if D2LIM[0] < 3:
                            return
                        g.wait_ge(sems["sv"], cnt["sv"])
                        wait_dma(g, "rdz", "gsc")
                        for j in range(8):
                            inc(g.indirect_dma_start(
                                out=RD_flat,
                                out_offset=IndirectOffsetOnAxis(
                                    seli.ap()[:, hh * 8 + j: hh * 8 + j + 1], 0),
                                in_=selw.ap()[:, hh * 8 + j: hh * 8 + j + 1],
                                in_offset=None,
                                element_offset=(SOFF + tb * 128) * NEXP,
                                compute_op=ALU.add), "gsc", 16)
                    G_(_)

                if s == 0 and "sele" in tap_out:
                    def _(v, tb=tb):
                        v.wait_ge(sems["go"], cnt["go"])
                        wait_dma(v, "gsc", "tap")
                        v.tensor_copy(out=lsb.ap()[:, 0:32], in_=seli.ap())
                        v.tensor_copy(out=lsb.ap()[:, 32:64], in_=selw.ap())
                    V(_)

                    def _(sync, tb=tb):
                        sync.wait_ge(sems["vt"], cnt["vt"])
                        inc(sync.dma_start(out=tap_out["sele"][:, tb * 32:(tb + 1) * 32],
                                           in_=lsb.ap()[:, 0:32]), "tap", 16)
                        inc(sync.dma_start(out=tap_out["selwt"][:, tb * 32:(tb + 1) * 32],
                                           in_=lsb.ap()[:, 32:64]), "tap", 16)
                    S(_)
            barrier()
            if STAGE[0] < 7:
                return

            for tg in range(4):
                TG0 = tg * 512
                for io in range(32):
                    def _(g, io=io):
                        g.wait_ge(sems["ev"], cnt["ev"] - 1 if io else cnt["ev"])
                        load_lhsT(g, wsl[io % 2], WUPT, io * 128)
                    G_(_)

                    def _(t, io=io, TG0=TG0):
                        wait_dma(t, ["wsl0", "wsl1"][io % 2])
                        t.wait_ge(sems["sv"], cnt["sv"] - 1 if io else cnt["sv"])
                        pq = [ps_a, ps_b][io % 2]
                        last = None
                        for c in range(NCH):
                            last = t.matmul(pq.ap(),
                                            wsl[io % 2].ap()[:, c * 128:(c + 1) * 128],
                                            fmv(ffm, c, TG0, 512),
                                            start=(c == 0), stop=(c == NCH - 1))
                        inc(last, "ev")
                    T(_)

                    def _(sc, io=io):
                        sc.wait_ge(sems["ev"], cnt["ev"])
                        pq = [ps_a, ps_b][io % 2]
                        sc.activation(out=hfd.ap()[:, io * 512:(io + 1) * 512],
                                          in_=pq.ap(), func=AF.Silu)
                    SC(_)

                for dh in range(2):
                    def _(g, dh=dh):
                        g.wait_ge(sems["ev"], cnt["ev"])
                        inc(g.dma_start(
                            out=wdnr.ap().rearrange("p (c n) -> p c n", n=512),
                            in_=WDNT[:, dh * 512:(dh + 1) * 512]
                            .rearrange("(c p) n -> p c n", p=128)), "wdnr", 16)
                    G_(_)

                    for tbl in range(4):
                        tb = tg * 4 + tbl
                        RT0 = SOFF + tb * 128

                        def _(g, RT0=RT0):
                            g.wait_ge(sems["vt"], cnt["vt"])
                            wait_dma(g, "gsc")
                            inc(g.dma_start(out=rlb.ap(), in_=RDD[RT0:RT0 + 128, :]),
                                "rlb", 16)
                        G_(_)

                        for ec in range(32):
                            pst = [ps_t0, ps_t1][ec % 2]

                            def _(t, ec=ec, pst=pst):
                                wait_dma(t, "rlb")
                                if ec < 2:
                                    t.wait_ge(sems["vt"], cnt["vt"])
                                else:
                                    t.wait_ge(sems["vt"], cnt["vt"] - 1)
                                inc(t.transpose(pst.ap()[:, 0:128],
                                                rlb.ap()[:, ec * 128:(ec + 1) * 128],
                                                identb.ap()), "ev")
                            T(_)

                            def _(v, ec=ec, pst=pst):
                                v.wait_ge(sems["ev"], cnt["ev"])
                                v.tensor_copy(out=rtt.ap()[:, ec * 128:(ec + 1) * 128],
                                                  in_=pst.ap()[:, 0:128])
                            V(_)

                        def _(sync, tb=tb, dh=dh):
                            sync.wait_ge(sems["vt"], cnt["vt"])
                            wait_dma(sync, "out")
                            inc(sync.dma_start(out=xe1.ap()[:, 0:512],
                                               in_=X1_D[tb * 128:(tb + 1) * 128,
                                                        dh * 512:(dh + 1) * 512]),
                                "xe1", 16)
                        S(_)

                        def _(t, tbl=tbl, dh=dh):
                            t.wait_ge(sems["vt"], cnt["vt"])
                            t.wait_ge(sems["sv"], cnt["sv"])
                            wait_dma(t, "wdnr", "ue")
                            pq = psq[tbl % 2]
                            for ic in range(32):
                                t.matmul(pq, hfd.ap()[:, ic * 512 + tbl * 128:
                                                      ic * 512 + tbl * 128 + 128],
                                         wdnr.ap()[:, ic * 512:(ic + 1) * 512],
                                         start=(ic == 0), stop=False)
                            last = None
                            for ec in range(32):
                                last = t.matmul(pq, rtt.ap()[:, ec * 128:(ec + 1) * 128],
                                                ue_s.ap()[:, ec * 1024 + dh * 512:
                                                          ec * 1024 + dh * 512 + 512],
                                                start=False, stop=(ec == 31))
                            inc(last, "ev")
                        T(_)

                        def _(v, tbl=tbl):
                            v.wait_ge(sems["ev"], cnt["ev"])
                            wait_dma(v, "xe1", "out")
                            pq = psq[tbl % 2]
                            v.tensor_tensor(out=xe2.ap()[:, 0:512], in0=pq,
                                                in1=xe1.ap()[:, 0:512], op=ALU.add)
                        V(_)

                        def _(sync, RT0=RT0, dh=dh):
                            sync.wait_ge(sems["vt"], cnt["vt"])
                            inc(sync.dma_start(out=OUT[RT0:RT0 + 128,
                                                       dh * 512:(dh + 1) * 512],
                                               in_=xe2.ap()[:, 0:512]), "out", 16)
                        S(_)
            barrier()

        for s in range(min(NSEQ, SEQLIM[0])):
            SOFF = s * TSEQ
            if STAGE[0] < 1:
                break

            for i in range(NT):
                R0 = SOFF + i * 128

                def _(sync, R0=R0):
                    sync.wait_ge(sems["vt"], cnt["vt"])
                    inc(sync.dma_start(out=xa.ap(), in_=X0[R0:R0 + 128, :]), "xa", 16)
                S(_)
                ln_stats(xa.ap(), xsq.ap(), pre="xa")

                def _(v):
                    v.wait_ge(sems["sv"], cnt["sv"])
                    wait_dma(v, "nsp")
                    v.tensor_scalar(out=xb.ap(), in0=xa.ap(), scalar1=st1.ap(),
                                        scalar2=st3.ap(), op0=ALU.subtract,
                                        op1=ALU.mult)
                V(_)

                def _(sync, R0=R0):
                    sync.wait_ge(sems["vt"], cnt["vt"])
                    inc(sync.dma_start(out=NIN_D[R0:R0 + 128, :], in_=xb.ap()), "nsp", 16)
                S(_)
                ln_stats(xb.ap(), xsq.ap())

                def _(v):
                    v.wait_ge(sems["sv"], cnt["sv"])
                    v.tensor_scalar(out=zt.ap(), in0=xb.ap(), scalar1=st1.ap(),
                                        scalar2=st3.ap(), op0=ALU.subtract,
                                        op1=ALU.mult)
                V(_)
                transpose_to_fm(zt.ap(), normed, i * 128, g1c, b1c)
            barrier()
            if s == 0:
                tap_bf16("normed", normed, NCH * TSEQ)
            if STAGE[0] < 2:
                continue

            def _(v):
                v.memset(vtm.ap(), 1.0)
            V(_)

            for fo in range(16):
                slot = wsl[fo % 2]

                def _(g, slot=slot):
                    g.wait_ge(sems["ev"], max(0, cnt["ev"] - 1))
                    load_lhsT(g, slot, WQKV, fo * 128)
                G_(_)

                for tb in range(4):
                    def _(t, slot=slot, tb=tb):
                        wait_dma(t, ssem(slot))
                        if tb < 2:
                            t.wait_ge(sems["vt"], cnt["vt"])
                        else:
                            t.wait_ge(sems["vt"], cnt["vt"] - 1)
                        last = None
                        for c in range(NCH):
                            last = t.matmul([ps_a, ps_b][tb % 2].ap(),
                                            slot.ap()[:, c * 128:(c + 1) * 128],
                                            fmv(normed, c, tb * 512, 512),
                                            start=(c == 0), stop=(c == NCH - 1))
                        inc(last, "ev")
                    T(_)

                    def _(v, fo=fo, tb=tb):
                        dstbuf = qf if fo < 8 else kf
                        c_out = fo % 8
                        scale = 0.125 if fo < 8 else 1.0
                        v.wait_ge(sems["ev"], cnt["ev"])
                        v.tensor_scalar_mul(out=fmv(dstbuf, c_out, tb * 512, 512),
                                                in0=[ps_a, ps_b][tb % 2].ap(),
                                                scalar1=scale)
                    V(_)
            barrier()
            if s == 0:
                tap_bf16("qf", qf, NCH * TSEQ)
                tap_bf16("kf", kf, NCH * TSEQ)

            def _(g):
                load_rhs_blk(g, wv0, WQKV, 2048)
                load_rhs_blk(g, wv1, WQKV, 2048 + 512)
            G_(_)

            for i in range(NT):
                for vh in range(2):
                    slot = [wv0, wv1][vh]

                    def _(t, i=i, vh=vh, slot=slot):
                        wait_dma(t, ssem(slot))
                        t.wait_ge(sems["sv"], cnt["sv"])
                        t.wait_ge(sems["vt"], cnt["vt"] - 1 if (i or vh) else cnt["vt"])
                        last = None
                        for c in range(NCH):
                            last = t.matmul([ps_a, ps_b][vh].ap(),
                                            fmv(normed, c, i * 128, 128),
                                            slot.ap()[:, c * 512:(c + 1) * 512],
                                            start=(c == 0), stop=(c == NCH - 1))
                        inc(last, "ev")
                    T(_)

                def _(v, i=i):
                    v.wait_ge(sems["ev"], cnt["ev"])
                    base = i * NH * 65
                    last = None
                    for vh in range(2):
                        pq = [ps_a, ps_b][vh]
                        dst = vtm.ap()[:, base + vh * 8 * 65: base + (vh * 8 + 8) * 65] \
                            .rearrange("p (h j) -> p h j", j=65)[:, :, 0:64]
                        last = v.tensor_copy(out=dst,
                                             in_=pq.ap().rearrange("p (h j) -> p h j",
                                                                   j=64))
                    pass
                V(_)

                def _(t, i=i):
                    t.wait_ge(sems["vt"], cnt["vt"])
                    for c in range(NCH):
                        t.matmul(ps_a.ap()[:, 0:2], fmv(normed, c, i * 128, 128),
                                 wr_s.ap()[:, c * 2:(c + 1) * 2],
                                 start=(c == 0), stop=False)
                    inc(t.matmul(ps_a.ap()[:, 0:2], ones_b.ap()[0:1, 0:128], brr.ap(),
                                 start=False, stop=True), "ev")
                T(_)

                def _(sc, i=i):
                    sc.wait_ge(sems["ev"], cnt["ev"])
                    sc.activation(out=al_c.ap()[:, i:i + 1], in_=ps_a.ap()[:, 0:1],
                                  func=AF.Sigmoid)
                    sc.activation(out=st1.ap(), in_=ps_a.ap()[:, 1:2], func=AF.Sigmoid,
                                  bias=sbb.ap()[:, 0:1])
                    sc.activation(out=st2.ap(), in_=ps_a.ap()[:, 1:2], func=AF.Sigmoid,
                                  bias=sbb.ap()[:, 1:2])
                    sc.activation(out=st3.ap(), in_=st1.ap(), func=AF.Relu,
                                      scale=m1c.ap(), bias=onec.ap())
                SC(_)

                def _(v, i=i):
                    v.wait_ge(sems["sv"], cnt["sv"])
                    v.tensor_tensor(out=st2.ap(), in0=st2.ap(), in1=st3.ap(), op=ALU.mult)
                    v.scalar_tensor_tensor(out=st5.ap(), in0=st2.ap(), scalar=-1.0,
                                           in1=st3.ap(), op0=ALU.mult, op1=ALU.add)
                    v.tensor_relu(out=st5.ap(), in_=st5.ap())
                    v.tensor_add(out=st1.ap(), in0=st1.ap(), in1=st2.ap())
                    v.tensor_add(out=ws_c.ap()[:, i:i + 1], in0=st1.ap(),
                                     in1=st5.ap())
                V(_)
            barrier()
            if s == 0:
                tap_f32("ws", ws_c, NT)
                tap_f32("alpha", al_c, NT)

            for h in range(NH):
                ch, rb = h // 2, 64 * (h % 2)
                for qt in range(NT):
                    def _(t, qt=qt, ch=ch, rb=rb):
                        t.wait_ge(sems["vt"], cnt["vt"])
                        last = None
                        for kb in range(4):
                            last = t.matmul(
                                psq[kb],
                                qf.ap()[rb:rb + 64, ch * TSEQ + qt * 128:
                                        ch * TSEQ + qt * 128 + 128],
                                kf.ap()[rb:rb + 64, ch * TSEQ + kb * 512:
                                        ch * TSEQ + (kb + 1) * 512],
                                start=True, stop=True)
                        inc(last, "ev")
                    T(_)

                    def _(v, qt=qt):
                        v.wait_ge(sems["ev"], cnt["ev"])
                        v.reduce_max(out=st4w.ap(),
                                     in_=ps_big.ap().rearrange("p (b n) -> p b n", n=512),
                                     axis=AX.X)
                        v.reduce_max(out=st1.ap(), in_=st4w.ap(), axis=AX.X)
                        v.tensor_scalar_mul(out=st1.ap(), in0=st1.ap(),
                                                scalar1=-1.0)
                    V(_)

                    def _(t, qt=qt):
                        t.wait_ge(sems["vt"], cnt["vt"])
                        inc(t.transpose(ps_a.ap()[0:1, 0:128], st1.ap(), identf.ap()),
                            "ev")
                    T(_)

                    def _(v, qt=qt):
                        v.wait_ge(sems["ev"], cnt["ev"])
                        v.tensor_copy(out=nmrow.ap()[0:1, qt * 128:(qt + 1) * 128],
                                          in_=ps_a.ap()[0:1, 0:128])
                    V(_)

                for qg in range(8):
                    q0 = qg * 256
                    for kc in range(16):
                        pq = [ps_a, ps_b][kc % 2]

                        def _(t, kc=kc, q0=q0, ch=ch, rb=rb, pq=pq):
                            if kc < 2:
                                t.wait_ge(sems["vt"], cnt["vt"])
                                t.wait_ge(sems["sv"], cnt["sv"])
                            else:
                                t.wait_ge(sems["sv"], cnt["sv"] - 1)
                            t.matmul(pq.ap()[:, 0:256],
                                     kf.ap()[rb:rb + 64, ch * TSEQ + kc * 128:
                                             ch * TSEQ + kc * 128 + 128],
                                     qf.ap()[rb:rb + 64, ch * TSEQ + q0:
                                             ch * TSEQ + q0 + 256],
                                     start=True, stop=False)
                            inc(t.matmul(pq.ap()[:, 0:256], ones_b.ap()[0:1, 0:128],
                                         nmrow.ap()[0:1, q0:q0 + 256],
                                         start=False, stop=True), "ev")
                        T(_)

                        def _(sc, kc=kc, pq=pq):
                            sc.wait_ge(sems["ev"], cnt["ev"])
                            sc.activation(out=expT.ap()[:, kc * 256:(kc + 1) * 256],
                                              in_=pq.ap()[:, 0:256], func=AF.Exp)
                        SC(_)

                    def _(t, h=h):
                        t.wait_ge(sems["sv"], cnt["sv"])
                        t.wait_ge(sems["vt"], cnt["vt"])
                        last = None
                        for kc in range(16):
                            last = t.matmul(
                                ps_big.ap()[0:65, 1024:1280],
                                vtm.ap()[:, kc * NH * 65 + h * 65:
                                         kc * NH * 65 + h * 65 + 65],
                                expT.ap()[:, kc * 256:(kc + 1) * 256],
                                start=(kc == 0), stop=(kc == 15))
                        inc(last, "ev")
                    T(_)

                    def _(v):
                        v.wait_ge(sems["ev"], cnt["ev"])
                        v.reciprocal(out=zrow.ap()[0:1, 0:256],
                                         in_=ps_big.ap()[64:65, 1024:1280])
                    V(_)

                    def _(t):
                        t.wait_ge(sems["vt"], cnt["vt"])
                        inc(t.matmul(ps_big.ap()[0:64, 1536:1792], ones_f.ap()[0:1, 0:64],
                                     zrow.ap()[0:1, 0:256], start=True, stop=True), "ev")
                    T(_)

                    def _(v, ch=ch, rb=rb, q0=q0):
                        v.wait_ge(sems["ev"], cnt["ev"])
                        v.tensor_copy(out=zbb.ap()[0:64, 0:256],
                                      in_=ps_big.ap()[0:64, 1536:1792])
                        v.tensor_tensor(
                            out=attnO.ap()[rb:rb + 64, ch * TSEQ + q0:
                                           ch * TSEQ + q0 + 256],
                            in0=ps_big.ap()[0:64, 1024:1280], in1=zbb.ap()[0:64, 0:256],
                            op=ALU.mult)
                    V(_)
            barrier()
            if s == 0:
                tap_bf16("attnO", attnO, NCH * TSEQ)
            if STAGE[0] < 3:
                continue

            emit_c_phases(s, SOFF)
            if STAGE[0] < 5:
                continue
            emit_d_phases(s, SOFF)

        barrier()
    return nc

_BF = ml_dtypes.bfloat16


def _prep_inputs(inputs):
    f32 = np.float32
    g = lambda k: np.asarray(inputs[k], dtype=f32)

    x = g("L0_attn_x").reshape(32 * 2048, 1024)
    com = {}
    com["wr"] = np.ascontiguousarray(g("L0_attn_Wr"))
    com["brrow"] = np.ascontiguousarray(g("L0_attn_br").reshape(1, 2))
    com["sbrow"] = np.ascontiguousarray(g("L0_attn_step_bias").reshape(1, 3))
    for nm, key in [("g1c", "L0_attn_ln1_g"), ("b1c", "L0_attn_ln1_b"),
                    ("g2c", "L0_attn_ln2_g"), ("b2c", "L0_attn_ln2_b"),
                    ("cbc", "L0_attn_conv_b")]:
        com[nm] = np.ascontiguousarray(g(key).reshape(NCH, 128).T)
    com["ls1r"] = np.ascontiguousarray(g("L0_attn_ls1").reshape(1, D))
    com["ls2r"] = np.ascontiguousarray(g("L0_attn_ls2").reshape(1, D))
    cw = g("L0_attn_conv_w").reshape(D, 4)
    com["cwt"] = np.ascontiguousarray(cw.reshape(NCH, 128, 4).transpose(1, 0, 2)
                                      .reshape(128, NCH * 4))
    com["wqkv"] = np.ascontiguousarray(g("L0_attn_Wqkv"))
    com["wout"] = np.ascontiguousarray(g("L0_attn_Wout"))
    com["wgate"] = np.ascontiguousarray(g("L0_attn_Wgate"))
    com["win"] = np.ascontiguousarray(g("L0_attn_Win_ssm"))
    com["wossm"] = np.ascontiguousarray(g("L0_attn_Wout_ssm"))
    com["wg8"] = np.ascontiguousarray(g("L0_attn_Wg"))
    com["w1"] = np.ascontiguousarray(g("L0_attn_W1"))
    com["w3"] = np.ascontiguousarray(g("L0_attn_W3"))
    com["w2"] = np.ascontiguousarray(g("L0_attn_W2"))
    com["wupT"] = np.ascontiguousarray(g("L0_ffn_W_up").T)
    com["wdnT"] = np.ascontiguousarray(g("L0_ffn_W_down").T)
    com["wqT"] = np.ascontiguousarray(g("L0_ffn_W_q").T)
    com["deT"] = np.ascontiguousarray(g("L0_ffn_down_embed").T)
    com["ue"] = np.ascontiguousarray(g("L0_ffn_up_embed"))
    keys = g("L0_ffn_keys")
    kq = keys.transpose(2, 0, 3, 1).reshape(2 * PH, NKEY, NKEY)
    kq = kq.transpose(1, 0, 2).reshape(NKEY, 2 * PH * NKEY)
    kqd = np.concatenate([kq, kq], axis=0)
    com["keysq"] = np.ascontiguousarray(kqd)
    com["identb"] = np.eye(128, dtype=_BF)
    com["identf"] = np.eye(128, dtype=f32)
    com["iotatl"] = np.arange(128, dtype=f32).reshape(128, 1)
    com["iota64"] = np.tile(np.arange(64, dtype=f32), (128, 1))
    in_maps = []
    for c in range(8):
        m = dict(com)
        m["x0"] = np.ascontiguousarray(x[c * TCORE:(c + 1) * TCORE])
        in_maps.append(m)
    return in_maps


def kernel(**inputs):
    nc = build_nc(taps=DEBUG_TAPS)
    in_maps = _prep_inputs(inputs)
    res = run_bass_kernel_spmd(nc, in_maps, core_ids=list(range(8)))
    if DEBUG_TAPS:
        kernel.last_results = res.results
    out = np.concatenate([r["out"] for r in res.results], axis=0)
    return out.reshape(32, 2048, 1024)
